# version 14
# baseline (speedup 1.0000x reference)
import numpy as np
import ml_dtypes
import concourse.bass as bass
import concourse.mybir as mybir
from concourse.bass_utils import run_bass_kernel_spmd
from contextlib import ExitStack

F32 = mybir.dt.float32
BF16 = mybir.dt.bfloat16
I8 = mybir.dt.int8
AF = mybir.ActivationFunctionType
ALU = mybir.AluOpType
AX = mybir.AxisListType

NCORES = 8
T = 1024
D = 4096
NT = T // 128
NC = D // 128

MAGIC = 12582912.0
TWO_PI = 6.283185307179586
CW_C1 = 6.28125
CW_C2 = 0.0019353071795864769


COMPUTE = ("tensor", "vector", "scalar", "gpsimd")
QUEUES = ("sync", "scalar", "gpsimd")


class Sched:
    def __init__(self, nc):
        self.nc = nc
        self.q = {e: [] for e in ("sync", "tensor", "vector", "scalar", "gpsimd")}
        self.cnt = {e: 0 for e in COMPUTE}
        self.dcnt = {}
        self.streams = []
        self.lw = {}
        self.rd = {}
        self.seen = {}

    def barrier(self):
        snap = {("c", e): self.cnt[e] for e in COMPUTE if self.cnt[e]}
        snap.update({("d", st): 16 * self.dcnt[st] for st in self.streams})
        self.bar = snap

    def _waits(self, ckey, reads, writes):
        seen = self.seen.setdefault(ckey, {})
        waits = {}

        def need(dep):
            if dep is None:
                return
            kind, who, dc = dep
            if kind == "c":
                if who == ckey == "tensor":
                    return
                val = dc
                key = ("c", who)
            else:
                val = 16 * self.dcnt[who]
                key = ("d", who)
            if seen.get(key, 0) >= val:
                return
            waits[key] = max(waits.get(key, 0), val)

        for key, val in getattr(self, "bar", {}).items():
            if seen.get(key, 0) < val:
                waits[key] = max(waits.get(key, 0), val)
        for r in reads:
            need(self.lw.get(r))
        for w in writes:
            need(self.lw.get(w))
            for pk, dc in self.rd.get(w, {}).items():
                need((pk[0], pk[1], dc))
        for k, v in waits.items():
            seen[k] = v
        return waits

    def _record(self, pkey, cnt, reads, writes):
        for r in reads:
            self.rd.setdefault(r, {})[pkey] = cnt
        for w in writes:
            self.lw[w] = (pkey[0], pkey[1], cnt)
            self.rd[w] = {}

    def op(self, eng, fn, reads=(), writes=(), tag=None):
        waits = self._waits(eng, reads, writes)
        self.cnt[eng] += 1
        self._record(("c", eng), self.cnt[eng], reads, writes)
        self.q[eng].append((waits, fn, ("c", eng), 1, (tag, tuple(reads), tuple(writes), fn.__code__.co_firstlineno)))

    def dma(self, stream, fn, reads=(), writes=(), queue="sync", tag=None):
        if stream not in self.dcnt:
            self.dcnt[stream] = 0
            self.streams.append(stream)
        waits = self._waits(queue if queue != "sync" else "syncq", reads, writes)
        self.dcnt[stream] += 1
        self._record(("d", stream), self.dcnt[stream], reads, writes)
        self.q[queue].append((waits, fn, ("d", stream), 16, (tag, tuple(reads), tuple(writes), fn.__code__.co_firstlineno)))

    def emit(self):
        nc = self.nc
        with ExitStack() as ctx:
            sems = {}
            for e in COMPUTE:
                sems[("c", e)] = ctx.enter_context(nc.semaphore(f"sem_{e}"))
            for st in self.streams:
                sems[("d", st)] = ctx.enter_context(nc.semaphore(f"sem_{st}"))
            with nc.Block() as blk:
                def body(e):
                    def _(engobj):
                        for waits, fn, pkey, amt, tag in self.q[e]:
                            for k, v in waits.items():
                                engobj.wait_ge(sems[k], v)
                            fn(engobj).then_inc(sems[pkey], amt)
                        if e == "sync":
                            for st in self.streams:
                                engobj.wait_ge(sems[("d", st)], 16 * self.dcnt[st])
                    _.__name__ = f"body_{e}"
                    return _
                for e in ("sync", "tensor", "vector", "scalar", "gpsimd"):
                    if self.q[e] or e == "sync":
                        getattr(blk, e)(body(e))



def ln_prepare(s, sx_ap, stats, mv, rstd, negmb, readkeys, k, ncols=D, col0=0):
    nchunks = (ncols + 511) // 512
    for c in range(nchunks):
        c0, c1 = col0 + c * 512, col0 + min(ncols, (c + 1) * 512)
        s.op("vector", lambda v, c=c, c0=c0, c1=c1: v.bn_stats(stats[:, c * 6:(c + 1) * 6], sx_ap[:, c0:c1]),
             reads=list(readkeys), writes=[(k, "st", c)])
    s.op("vector", lambda v: v.bn_aggr(mv[:], stats[:, 0:6 * nchunks]),
         reads=[(k, "st", c) for c in range(nchunks)], writes=[(k, "mv")])
    s.op("vector", lambda v: v.tensor_scalar(out=rstd[:], in0=mv[:, 1:2], scalar1=1e-5, scalar2=None, op0=ALU.add),
         reads=[(k, "mv")], writes=[(k, "veps")])
    s.op("scalar", lambda sc: sc.activation(out=rstd[:], in_=rstd[:], func=AF.Sqrt),
         reads=[(k, "veps")], writes=[(k, "sd")])
    s.op("vector", lambda v: v.reciprocal(out=rstd[:], in_=rstd[:]),
         reads=[(k, "sd")], writes=[(k, "rstd")])
    s.op("vector", lambda v: v.tensor_tensor(out=negmb[:], in0=mv[:, 0:1], in1=rstd[:], op=ALU.mult),
         reads=[(k, "mv"), (k, "rstd")], writes=[(k, "mb")])
    s.op("vector", lambda v: v.tensor_scalar(out=negmb[:], in0=negmb[:], scalar1=-1.0, scalar2=None, op0=ALU.mult),
         reads=[(k, "mb")], writes=[(k, "negmb")])


def ln_apply(s, out_ap, in_ap, rstd, negmb, k, readkeys, outkey):
    s.op("scalar", lambda sc: sc.activation(out=out_ap, in_=in_ap, func=AF.Identity,
                                            bias=negmb[:, 0:1], scale=rstd[:, 0:1]),
         reads=list(readkeys) + [(k, "rstd"), (k, "negmb")], writes=[outkey])


class TPU:
    _n = 0

    def __init__(self, s, nc, ctx, nbanks=2):
        TPU._n += 1
        self.name = f"tpu{TPU._n}"
        self.ident = ctx.enter_context(nc.sbuf_tensor(f"{self.name}_id", [128, 128], BF16))
        self.colm = ctx.enter_context(nc.sbuf_tensor(f"{self.name}_cm", [128, 128], F32))
        self.rowi = ctx.enter_context(nc.sbuf_tensor(f"{self.name}_ri", [128, 1], F32))
        self.nbanks = nbanks
        self.ps = [ctx.enter_context(nc.psum_tensor(f"{self.name}_ps{i}", [128, 1024], BF16))
                   for i in range(nbanks)]
        self.n = 0
        self.s = s
        k = self.name
        s.op("gpsimd", lambda g: g.iota(self.colm[:], [[1, 128]], channel_multiplier=0,
                                        allow_small_or_imprecise_dtypes=True), writes=[(k, "cm")])
        s.op("gpsimd", lambda g: g.iota(self.rowi[:], [[0, 1]], channel_multiplier=1,
                                        allow_small_or_imprecise_dtypes=True), writes=[(k, "ri")])
        s.op("vector", lambda v: v.tensor_scalar(out=self.ident[:], in0=self.colm[:],
                                                 scalar1=self.rowi[:, 0:1], scalar2=None, op0=ALU.is_equal),
             reads=[(k, "cm"), (k, "ri")], writes=[(k, "id")])

    def t(self, out_ap, in_ap, reads, writes, evac="vector", p=None):
        s, k = self.s, self.name
        b = self.n % self.nbanks
        sl = (self.n // self.nbanks) % 8
        self.n += 1
        pdim = 128 if p is None else p
        ident = self.ident[:] if pdim == 128 else self.ident[0:pdim, 0:pdim]
        dst = self.ps[b][:, sl * 128:sl * 128 + pdim]
        s.op("tensor", lambda te, dst=dst: te.transpose(dst, in_ap, ident),
             reads=list(reads) + [(k, "id")], writes=[(k, "s", b, sl), (k, "bk", b)])
        if evac == "vector":
            s.op("vector", lambda e, dst=dst: e.tensor_copy(out=out_ap, in_=dst),
                 reads=[(k, "s", b, sl), (k, "bk", b)], writes=list(writes))
        else:
            s.op("scalar", lambda e, dst=dst: e.copy(out=out_ap, in_=dst),
                 reads=[(k, "s", b, sl), (k, "bk", b)], writes=list(writes))



def build_l0qkv():
    nc = bass.Bass("TRN2", target_bir_lowering=False, debug=False)
    x = nc.dram_tensor("x", [T, D], F32, kind="ExternalInput").ap()
    wqkv = nc.dram_tensor("wqkv", [D, 3072], F32, kind="ExternalInput").ap()
    bqkv = nc.dram_tensor("bqkv", [3072], F32, kind="ExternalInput").ap()
    qT = nc.dram_tensor("qT", [1024, T], BF16, kind="ExternalOutput").ap()
    kT = nc.dram_tensor("kT", [1024, T], BF16, kind="ExternalOutput").ap()
    vtok = nc.dram_tensor("vtok", [T, 1024], BF16, kind="ExternalOutput").ap()

    wq_r = wqkv.rearrange("(cc p) f -> p cc f", p=128)
    qT_r = qT.rearrange("(f p) t -> p f t", p=128)
    kT_r = kT.rearrange("(f p) t -> p f t", p=128)

    s = Sched(nc)
    with ExitStack() as ctx:
        sb = lambda *a: ctx.enter_context(nc.sbuf_tensor(*a))
        ps = lambda *a: ctx.enter_context(nc.psum_tensor(*a))
        tpu = TPU(s, nc, ctx)
        xt = sb("xt", [128, D], F32)
        stats = sb("stats", [128, 48], F32)
        mv = sb("mv", [128, 2], F32)
        rstd = sb("rstd", [128, 1], F32)
        negmb = sb("negmb", [128, 1], F32)
        xn_bf2 = [sb(f"xn_bf{i}", [128, D], BF16) for i in range(2)]
        xnT = sb("xnT", [128, NC * T], BF16)
        bq = sb("bq", [128, 24], F32)
        wstage = [sb(f"wstage{i}", [128, NC * 128], F32) for i in range(2)]
        wbf = [sb(f"wbf{i}", [128, NC * 128], BF16) for i in range(2)]
        outst = [sb(f"outst{i}", [128, T], BF16) for i in range(2)]
        vtt = sb("vtt", [128, 8 * 128], BF16)
        pmm = [ps(f"pmm{i}", [128, 512], F32) for i in range(2)]

        for f in range(24):
            s.dma("bq", lambda sy, f=f: sy.dma_start(out=bq[:, f:f + 1], in_=bqkv[f * 128:(f + 1) * 128].unsqueeze(-1)),
                  writes=[("bq", f)])

        for t in range(NT):
            s.dma("x", lambda sy, t=t: sy.dma_start(out=xt[:], in_=x[t * 128:(t + 1) * 128, :]),
                  writes=[("xt",)])
            ln_prepare(s, xt, stats, mv, rstd, negmb, [("xt",)], ("ln", t))
            ln_apply(s, xn_bf2[t % 2][:], xt[:], rstd, negmb, ("ln", t), [("xt",)], ("xn_bf", t % 2))
            for cc in range(NC):
                tpu.t(xnT[:, cc * T + t * 128: cc * T + (t + 1) * 128],
                      xn_bf2[t % 2][:, cc * 128:(cc + 1) * 128],
                      reads=[("xn_bf", t % 2)], writes=[("xnT", t, cc)],
                      evac="scalar" if cc % 2 else "vector")

        xnT_ready = [("xnT", t, cc) for t in range(NT) for cc in range(NC)]

        for f in range(24):
            p = f % 2
            s.dma(f"w{p}", lambda sy, f=f, p=p: sy.dma_start(out=wstage[p][:], in_=wq_r[:, :, f * 128:(f + 1) * 128]),
                  writes=[("wstage", p)])
            s.op("scalar", lambda sc, p=p: sc.copy(out=wbf[p][:], in_=wstage[p][:]),
                 reads=[("wstage", p)], writes=[("wbf", p)])
            for cc in range(NC):
                for th in range(2):
                    s.op("tensor", lambda te, p=p, th=th, cc=cc: te.matmul(
                            pmm[th][:], wbf[p][:, cc * 128:(cc + 1) * 128],
                            xnT[:, cc * T + th * 512: cc * T + (th + 1) * 512],
                            start=(cc == 0), stop=(cc == NC - 1)),
                         reads=[("wbf", p)] + ([] if (cc or th or f) else xnT_ready),
                         writes=[("pmm", th)], tag=f"mm{f}")
            for th in range(2):
                s.op("scalar", lambda sc, f=f, th=th, p=p: sc.activation(
                        out=outst[p][:, th * 512:(th + 1) * 512], in_=pmm[th][:],
                        func=AF.Identity, bias=bq[:, f:f + 1]),
                     reads=[("pmm", th), ("bq", f)], writes=[("outst", p, th)])
            if f < 8:
                s.dma("st", lambda sy, f=f, p=p: sy.dma_start(out=qT_r[:, f, :], in_=outst[p][:]),
                      reads=[("outst", p, 0), ("outst", p, 1)], writes=[], queue="scalar")
            elif f < 16:
                s.dma("st", lambda sy, f=f, p=p: sy.dma_start(out=kT_r[:, f - 8, :], in_=outst[p][:]),
                      reads=[("outst", p, 0), ("outst", p, 1)], writes=[], queue="scalar")
            else:
                dd = f - 16
                for tt in range(NT):
                    tpu.t(vtt[:, tt * 128:(tt + 1) * 128], outst[p][:, tt * 128:(tt + 1) * 128],
                          reads=[("outst", p, 0), ("outst", p, 1)], writes=[("vtt", tt)],
                          evac="scalar" if tt % 2 else "vector")
                    s.dma("vt", lambda sy, tt=tt, dd=dd: sy.dma_start(
                            out=vtok[tt * 128:(tt + 1) * 128, dd * 128:(dd + 1) * 128],
                            in_=vtt[:, tt * 128:(tt + 1) * 128]),
                          reads=[("vtt", tt)], writes=[], queue="scalar")
        s.emit()
    return nc



def build_l0attn(debug=False, nw=16):
    nc = bass.Bass("TRN2", target_bir_lowering=False, debug=False)
    x = nc.dram_tensor("x", [T, D], F32, kind="ExternalInput").ap()
    qTd = nc.dram_tensor("qT", [1024, T], BF16, kind="ExternalInput").ap()
    kTf = nc.dram_tensor("kTf", [1024, 8192], BF16, kind="ExternalInput").ap()
    vf = nc.dram_tensor("vf", [8192, 1024], BF16, kind="ExternalInput").ap()
    wo = nc.dram_tensor("wo", [1024, D], F32, kind="ExternalInput").ap()
    bo = nc.dram_tensor("bo", [D], F32, kind="ExternalInput").ap()
    y = nc.dram_tensor("y", [T, D], F32, kind="ExternalOutput").ap()
    if debug:
        dbgO = nc.dram_tensor("dbgO", [128, 8 * 1024], F32, kind="ExternalOutput").ap()
        dbgS = nc.dram_tensor("dbgS", [128, 32], F32, kind="ExternalOutput").ap()
    if debug == 3:
        dbgP = nc.dram_tensor("dbgP", [128, 512], BF16, kind="ExternalOutput").ap()
        dbgPT = nc.dram_tensor("dbgPT", [128, 512], BF16, kind="ExternalOutput").ap()
        dbgO7 = nc.dram_tensor("dbgO7", [128, 1024], F32, kind="ExternalOutput").ap()
    if debug == 2:
        dbgA = nc.dram_tensor("dbgA", [128, 4096], F32, kind="ExternalOutput").ap()
        dbgL = nc.dram_tensor("dbgL", [128, 4096], F32, kind="ExternalOutput").ap()
        dbgO7 = nc.dram_tensor("dbgO7", [128, 1024], F32, kind="ExternalOutput").ap()
        dbgST = nc.dram_tensor("dbgST", [128, 8], F32, kind="ExternalOutput").ap()

    kT_r = kTf.rearrange("(dd p) k -> p dd k", p=128)
    v_r = vf.rearrange("(kt p) d -> p kt d", p=128)
    qT_r = qTd.rearrange("(dd p) t -> p dd t", p=128)
    wo_r = wo.rearrange("(dd p) c -> p dd c", p=128)

    SCALE = 1.0 / 32.0
    NW = nw

    s = Sched(nc)
    with ExitStack() as ctx:
        sb = lambda *a: ctx.enter_context(nc.sbuf_tensor(*a))
        ps = lambda *a: ctx.enter_context(nc.psum_tensor(*a))
        qT_sb = sb("qT_sb", [128, 8, 1024], BF16)
        O_sb = sb("O_sb", [128, 8, 1024], F32)
        m_sb = sb("m_sb", [128, 8], F32)
        mn_sb = sb("mn_sb", [128, 8], F32)
        raw_sb = sb("raw_sb", [128, 8], F32)
        dl_sb = sb("dl_sb", [128, 8], F32)
        ng_sb = sb("ng_sb", [128, 8], F32)
        corr_sb = sb("corr_sb", [128, 8], F32)
        sum_sb = sb("sum_sb", [128, 8], F32)
        rs_sb = sb("rs_sb", [128, 8], F32)
        rsi_sb = sb("rsi_sb", [128, 8], F32)
        wo_bf = sb("wo_bf", [128, 8, 4096], BF16)
        bo_bcast = sb("bo_bcast", [128, 4096], F32)
        bo_row = sb("bo_row", [1, 4096], F32)
        ones1 = sb("ones1", [1, 128], F32)
        psS = [ps(f"psS{i}", [128, 512], F32) for i in range(2)]
        psO = [[ps(f"psO{i}_{j}", [128, 512], F32) for j in range(2)] for i in range(2)]
        tpu = TPU(s, nc, ctx)

        s.op("vector", lambda v: v.memset(m_sb[:], -30000.0), writes=[("m", qq) for qq in range(8)])
        s.op("vector", lambda v: v.memset(sum_sb[:], 0.0), writes=[("sum", qq) for qq in range(8)])
        s.op("vector", lambda v: v.memset(O_sb[:], 0.0), writes=[("O", qq) for qq in range(8)])
        s.op("vector", lambda v: v.memset(ones1[:], 1.0), writes=[("ones1",)])
        s.dma("cst", lambda sy: sy.dma_start(out=bo_row[:], in_=bo.unsqueeze(0)), writes=[("bo_row",)])
        s.dma("cst", lambda sy: sy.dma_start(out=qT_sb[:], in_=qT_r[:]), writes=[("qT",)])
        for ch in range(8):
            s.op("tensor", lambda te, ch=ch: te.matmul(psO[0][0][:], ones1[:], bo_row[:, ch * 512:(ch + 1) * 512],
                                                       start=True, stop=True),
                 reads=[("ones1",), ("bo_row",)], writes=[("psO", 0, 0)])
            s.op("vector", lambda v, ch=ch: v.tensor_copy(out=bo_bcast[:, ch * 512:(ch + 1) * 512], in_=psO[0][0][:]),
                 reads=[("psO", 0, 0)], writes=[("bo_bc", ch)])
        with ExitStack() as fctx:
            fsb = lambda *a: fctx.enter_context(nc.sbuf_tensor(*a))
            wstage = fsb("wstage", [128, 4096], F32)
            kwin = [fsb(f"kwin{i}", [128, 8, 512], BF16) for i in range(2)]
            vwin = [fsb(f"vwin{i}", [128, 4, 1024], BF16) for i in range(2)]
            p_sb = [fsb(f"p_sb{i}", [128, 512], BF16) for i in range(3)]
            pT_sb = [fsb(f"pT_sb{i}", [128, 512], BF16) for i in range(3)]

            def load_win(kw):
                b = kw % 2
                s.dma(f"kv{b}", lambda sy, kw=kw, b=b: sy.dma_start(
                        out=kwin[b][:], in_=kT_r[:, :, kw * 512:(kw + 1) * 512]),
                      writes=[("kwin", b)], queue="scalar")
                s.dma(f"kv{b}", lambda sy, kw=kw, b=b: sy.dma_start(
                        out=vwin[b][:], in_=v_r[:, kw * 4:(kw + 1) * 4, :]),
                      writes=[("vwin", b)], queue="scalar")

            for dd in range(8):
                s.dma("wo", lambda sy, dd=dd: sy.dma_start(out=wstage[:], in_=wo_r[:, dd, :]),
                      writes=[("wstage",)])
                s.op("scalar", lambda sc, dd=dd: sc.copy(out=wo_bf[:, dd, :], in_=wstage[:]),
                     reads=[("wstage",)], writes=[("wo_bf", dd)])
            load_win(0)
            pend = []
            pend_tp = []

            def emit_av(idx, qq, wb):
                ib = idx % 3
                po = idx % 2
                for kt in range(4):
                    for dh in range(2):
                        s.op("tensor", lambda te, po=po, ib=ib, wb=wb, dh=dh, kt=kt: te.matmul(
                                psO[po][dh][:], pT_sb[ib][:, kt * 128:(kt + 1) * 128],
                                vwin[wb][:, kt, dh * 512:(dh + 1) * 512],
                                start=(kt == 0), stop=(kt == 3)),
                             reads=[("pT", ib), ("vwin", wb)], writes=[("psO", po, dh)])
                for dh in range(2):
                    s.op("vector", lambda v, po=po, qq=qq, dh=dh: v.scalar_tensor_tensor(
                            out=O_sb[:, qq, dh * 512:(dh + 1) * 512], in0=O_sb[:, qq, dh * 512:(dh + 1) * 512],
                            scalar=corr_sb[:, qq:qq + 1], in1=psO[po][dh][:], op0=ALU.mult, op1=ALU.add),
                         reads=[("psO", po, dh), ("corr", qq), ("O", qq)], writes=[("O", qq)])

            def emit_tp(idx, ib):
                for kt in range(4):
                    tpu.t(pT_sb[ib][:, kt * 128:(kt + 1) * 128],
                          p_sb[ib][:, kt * 128:(kt + 1) * 128],
                          reads=[("p", ib)], writes=[("pT", ib)],
                          evac="vector" if kt % 2 else "scalar")

            for kw in range(NW):
                wb = kw % 2
                while pend_tp:
                    emit_tp(*pend_tp.pop(0))
                while pend:
                    emit_av(*pend.pop(0))
                if kw + 1 < NW:
                    load_win(kw + 1)
                for qq in range(8):
                    idx = kw * 8 + qq
                    ib = idx % 3
                    sb2 = idx % 2
                    for dd in range(8):
                        s.op("tensor", lambda te, sb2=sb2, wb=wb, dd=dd, qq=qq: te.matmul(
                                psS[sb2][:], qT_sb[:, dd, qq * 128:(qq + 1) * 128], kwin[wb][:, dd, :],
                                start=(dd == 0), stop=(dd == 7)),
                             reads=[("qT",), ("kwin", wb)], writes=[("psS", sb2)])
                    s.op("vector", lambda v, sb2=sb2, qq=qq: v.reduce_max(
                            out=raw_sb[:, qq:qq + 1], in_=psS[sb2][:], axis=AX.X),
                         reads=[("psS", sb2)], writes=[("raw", qq)])
                    s.op("vector", lambda v, qq=qq: v.tensor_scalar(
                            out=raw_sb[:, qq:qq + 1], in0=raw_sb[:, qq:qq + 1], scalar1=SCALE, scalar2=None, op0=ALU.mult),
                         reads=[("raw", qq)], writes=[("raw", qq)])
                    s.op("vector", lambda v, qq=qq: v.tensor_tensor(
                            out=mn_sb[:, qq:qq + 1], in0=m_sb[:, qq:qq + 1], in1=raw_sb[:, qq:qq + 1], op=ALU.max),
                         reads=[("raw", qq), ("m", qq)], writes=[("mn", qq)])
                    s.op("vector", lambda v, qq=qq: v.tensor_tensor(
                            out=dl_sb[:, qq:qq + 1], in0=m_sb[:, qq:qq + 1], in1=mn_sb[:, qq:qq + 1], op=ALU.subtract),
                         reads=[("m", qq), ("mn", qq)], writes=[("dl", qq)])
                    s.op("vector", lambda v, qq=qq: v.tensor_scalar(
                            out=ng_sb[:, qq:qq + 1], in0=mn_sb[:, qq:qq + 1], scalar1=-1.0, scalar2=None, op0=ALU.mult),
                         reads=[("mn", qq)], writes=[("ng", qq)])
                    s.op("vector", lambda v, qq=qq: v.tensor_copy(
                            out=m_sb[:, qq:qq + 1], in_=mn_sb[:, qq:qq + 1]),
                         reads=[("mn", qq), ("dl", qq)], writes=[("m", qq)])
                    s.op("scalar", lambda sc, qq=qq: sc.activation(
                            out=corr_sb[:, qq:qq + 1], in_=dl_sb[:, qq:qq + 1], func=AF.Exp),
                         reads=[("dl", qq)], writes=[("corr", qq)])
                    s.op("scalar", lambda sc, sb2=sb2, ib=ib, qq=qq: sc.activation(
                            out=p_sb[ib][:], in_=psS[sb2][:], func=AF.Exp, bias=ng_sb[:, qq:qq + 1],
                            scale=SCALE, accum_out=rs_sb[:, qq:qq + 1]),
                         reads=[("psS", sb2), ("ng", qq)], writes=[("p", ib), ("rs", qq)])
                    s.op("vector", lambda v, qq=qq: v.scalar_tensor_tensor(
                            out=sum_sb[:, qq:qq + 1], in0=sum_sb[:, qq:qq + 1], scalar=corr_sb[:, qq:qq + 1],
                            in1=rs_sb[:, qq:qq + 1], op0=ALU.mult, op1=ALU.add),
                         reads=[("rs", qq), ("corr", qq), ("sum", qq)], writes=[("sum", qq)])
                    pend_tp.append((idx, ib))
                    if len(pend_tp) > 1:
                        emit_tp(*pend_tp.pop(0))
                    pend.append((idx, qq, wb))
                    if len(pend) > 2:
                        emit_av(*pend.pop(0))
            while pend_tp:
                emit_tp(*pend_tp.pop(0))
            while pend:
                emit_av(*pend.pop(0))
            if debug == 3:
                s.dma("dbg", lambda sy: sy.dma_start(out=dbgP[:], in_=p_sb[1][:]), reads=[("p", 1)], writes=[])
                s.dma("dbg", lambda sy: sy.dma_start(out=dbgPT[:], in_=pT_sb[1][:]), reads=[("pT", 1)], writes=[])
                s.dma("dbg", lambda sy: sy.dma_start(out=dbgO7[:], in_=O_sb[:, 7, :]), reads=[("O", 7)], writes=[])

        s.barrier()
        if debug == 1:
            s.dma("dbg", lambda sy: sy.dma_start(out=dbgO[:], in_=O_sb[:].rearrange("p a b -> p (a b)")),
                  reads=[("O", qq) for qq in range(8)], writes=[])
            s.dma("dbg", lambda sy: sy.dma_start(out=dbgS[:, 0:8], in_=sum_sb[:]),
                  reads=[("sum", qq) for qq in range(8)], writes=[])
            s.dma("dbg", lambda sy: sy.dma_start(out=dbgS[:, 8:16], in_=m_sb[:]),
                  reads=[("m", qq) for qq in range(8)], writes=[])
            s.dma("dbg", lambda sy: sy.dma_start(out=dbgS[:, 16:24], in_=corr_sb[:]),
                  reads=[("corr", qq) for qq in range(8)], writes=[])
            s.dma("dbg", lambda sy: sy.dma_start(out=dbgS[:, 24:32], in_=rs_sb[:]),
                  reads=[("rs", qq) for qq in range(8)], writes=[])
        s.op("vector", lambda v: v.reciprocal(out=rsi_sb[:], in_=sum_sb[:]),
             reads=[("sum", qq) for qq in range(8)], writes=[("rsi",)])
        with ExitStack() as ectx:
            esb = lambda *a: ectx.enter_context(nc.sbuf_tensor(*a))
            o_bf = esb("o_bf", [128, 1024], BF16)
            oT_sb = esb("oT_sb", [128, 8, 128], BF16)
            A_sb = esb("A_sb", [128, 4096], F32)
            lnt = esb("lnt", [128, 4096], F32)
            xep = esb("xep", [128, 4096], F32)
            stats = esb("statsB", [128, 48], F32)
            mv = esb("mvB", [128, 2], F32)
            rstd = esb("rstdB", [128, 1], F32)
            negmb = esb("negmbB", [128, 1], F32)
            for qq in range(8):
                s.dma("x", lambda sy, qq=qq: sy.dma_start(out=xep[:], in_=x[qq * 128:(qq + 1) * 128, :]),
                      writes=[("xep",)])
                s.op("scalar", lambda sc, qq=qq: sc.activation(
                        out=o_bf[:], in_=O_sb[:, qq, :], func=AF.Identity, scale=rsi_sb[:, qq:qq + 1]),
                     reads=[("O", qq), ("rsi",)], writes=[("o_bf",)])
                for dd in range(8):
                    tpu.t(oT_sb[:, dd, :], o_bf[:, dd * 128:(dd + 1) * 128],
                          reads=[("o_bf",)], writes=[("oT", dd)],
                          evac="vector" if dd % 2 else "scalar")
                for ch in range(8):
                    cb = ch % 2
                    for dd in range(8):
                        s.op("tensor", lambda te, cb=cb, ch=ch, dd=dd: te.matmul(
                                psO[0][cb][:], oT_sb[:, dd, :], wo_bf[:, dd, ch * 512:(ch + 1) * 512],
                                start=(dd == 0), stop=(dd == 7)),
                             reads=[("oT", dd), ("wo_bf", dd)], writes=[("psO", 0, cb)])
                    s.op("vector", lambda v, cb=cb, ch=ch: v.tensor_tensor(
                            out=A_sb[:, ch * 512:(ch + 1) * 512], in0=psO[0][cb][:],
                            in1=bo_bcast[:, ch * 512:(ch + 1) * 512], op=ALU.add),
                         reads=[("psO", 0, cb), ("bo_bc", ch)], writes=[("A", ch)])
                ln_prepare(s, A_sb, stats, mv, rstd, negmb, [("A", ch) for ch in range(8)], ("lnB", qq))
                ln_apply(s, lnt[:], A_sb[:], rstd, negmb, ("lnB", qq),
                         [("A", ch) for ch in range(8)], ("lnt",))
                s.op("vector", lambda v, qq=qq: v.tensor_tensor(out=xep[:], in0=xep[:], in1=lnt[:], op=ALU.add),
                     reads=[("xep",), ("lnt",)], writes=[("xep",)])
                s.dma("st", lambda sy, qq=qq: sy.dma_start(out=y[qq * 128:(qq + 1) * 128, :], in_=xep[:]),
                      reads=[("xep",)], writes=[])
            if debug == 2:
                s.dma("dbg", lambda sy: sy.dma_start(out=dbgA[:], in_=A_sb[:]),
                      reads=[("A", ch) for ch in range(8)], writes=[])
                s.dma("dbg", lambda sy: sy.dma_start(out=dbgL[:], in_=lnt[:]), reads=[("lnt",)], writes=[])
                s.dma("dbg", lambda sy: sy.dma_start(out=dbgO7[:], in_=O_sb[:, 7, :]),
                      reads=[("O", 7)], writes=[])
                s.dma("dbg", lambda sy: sy.dma_start(out=dbgST[:], in_=rsi_sb[:]), reads=[("rsi",)], writes=[])
        s.emit()
    return nc



def build_l0ffn(debug=False):
    nc = bass.Bass("TRN2", target_bir_lowering=False, debug=False)
    yd = nc.dram_tensor("y", [T, D], F32, kind="ExternalInput").ap()
    rw = nc.dram_tensor("rw", [64, D], F32, kind="ExternalInput").ap()
    rb = nc.dram_tensor("rb", [64], F32, kind="ExternalInput").ap()
    eb = nc.dram_tensor("eb", [64, D], F32, kind="ExternalInput").ap()
    y2 = nc.dram_tensor("y2", [T, D], F32, kind="ExternalOutput").ap()
    if debug:
        dbgL = nc.dram_tensor("dbgL", [128, 64], F32, kind="ExternalOutput").ap()
        dbgSC = nc.dram_tensor("dbgSC", [128, 64], F32, kind="ExternalOutput").ap()
        dbgTM = nc.dram_tensor("dbgTM", [128, 64], F32, kind="ExternalOutput").ap()
        dbgAC = nc.dram_tensor("dbgAC", [128, 2], F32, kind="ExternalOutput").ap()
        dbgO = nc.dram_tensor("dbgO", [128, D], F32, kind="ExternalOutput").ap()

    K_TOP = 8
    s = Sched(nc)
    with ExitStack() as ctx:
        sb = lambda *a: ctx.enter_context(nc.sbuf_tensor(*a))
        ps = lambda *a: ctx.enter_context(nc.psum_tensor(*a))
        tpu = TPU(s, nc, ctx)
        rw_st = sb("rw_st", [64, D], F32)
        rw_bf = sb("rw_bf", [64, D], BF16)
        rwT = sb("rwT", [128, NC * 64], BF16)
        eb_st = sb("eb_st", [64, D], F32)
        eb_bf = sb("eb_bf", [64, D], BF16)
        rb_row = sb("rb_row", [1, 64], F32)
        rb_bc = sb("rb_bc", [128, 64], F32)
        ones1 = sb("ones1C", [1, 128], F32)
        yt = sb("ytC", [128, D], F32)
        xn32 = sb("xn32", [128, D], F32)
        xn_bf = sb("xn_bfC", [128, D], BF16)
        xnT = sb("xnTC", [128, D], BF16)
        lg = sb("lg", [128, 64], F32)
        ngm = sb("ngmC", [128, 1], F32)
        ex = sb("exC", [128, 64], F32)
        rsx = sb("rsxC", [128, 1], F32)
        sc_sb = sb("sc_sb", [128, 64], F32)
        wrk = sb("wrk", [128, 64], F32)
        msk = sb("mskC", [128, 64], F32)
        tmask = sb("tmaskC", [128, 64], F32)
        mx = sb("mxC", [128, 1], F32)
        acc = sb("accC", [128, 1], F32)
        wd32 = sb("wd32", [128, 64], F32)
        wd_bf = sb("wd_bf", [128, 128], BF16)
        wdT = sb("wdT", [128, 128], BF16)
        o_sb = sb("o_sbC", [128, D], F32)
        lnt = sb("lntC", [128, D], F32)
        stats = sb("statsC", [128, 48], F32)
        mv = sb("mvC", [128, 2], F32)
        rstd = sb("rstdC", [128, 1], F32)
        negmb = sb("negmbC", [128, 1], F32)
        psL = ps("psL", [128, 64], F32)
        psF = [ps(f"psF{i}", [128, 512], F32) for i in range(2)]

        s.op("vector", lambda v: v.memset(ones1[:], 1.0), writes=[("ones1",)])
        s.op("vector", lambda v: v.memset(wd_bf[:], 0.0), writes=[("wd_bf",)])
        s.dma("cst", lambda sy: sy.dma_start(out=rw_st[:], in_=rw[:]), writes=[("rw_st",)])
        s.dma("cst", lambda sy: sy.dma_start(out=eb_st[:], in_=eb[:]), writes=[("eb_st",)])
        s.dma("cst", lambda sy: sy.dma_start(out=rb_row[:], in_=rb.unsqueeze(0)), writes=[("rb_row",)])
        s.op("scalar", lambda sc: sc.copy(out=rw_bf[:], in_=rw_st[:]), reads=[("rw_st",)], writes=[("rw_bf",)])
        s.op("scalar", lambda sc: sc.copy(out=eb_bf[:], in_=eb_st[:]), reads=[("eb_st",)], writes=[("eb_bf",)])
        for cc in range(NC):
            tpu.t(rwT[:, cc * 64:(cc + 1) * 64], rw_bf[:, cc * 128:(cc + 1) * 128],
                  reads=[("rw_bf",)], writes=[("rwT", cc)], p=64,
                  evac="scalar" if cc % 2 else "vector")
        s.op("tensor", lambda te: te.matmul(psL[:], ones1[:], rb_row[:], start=True, stop=True),
             reads=[("ones1",), ("rb_row",)], writes=[("psL",)])
        s.op("vector", lambda v: v.tensor_copy(out=rb_bc[:], in_=psL[:]),
             reads=[("psL",)], writes=[("rb_bc",)])

        for t in range(NT):
            k = ("C", t)
            s.dma("x", lambda sy, t=t: sy.dma_start(out=yt[:], in_=yd[t * 128:(t + 1) * 128, :]),
                  writes=[("yt",)])
            ln_prepare(s, yt, stats, mv, rstd, negmb, [("yt",)], ("lnC", t))
            ln_apply(s, xn32[:], yt[:], rstd, negmb, ("lnC", t), [("yt",)], ("xn32",))
            s.op("scalar", lambda sc: sc.copy(out=xn_bf[:], in_=xn32[:]), reads=[("xn32",)], writes=[("xn_bf",)])
            for cc in range(NC):
                tpu.t(xnT[:, cc * 128:(cc + 1) * 128], xn_bf[:, cc * 128:(cc + 1) * 128],
                      reads=[("xn_bf",)], writes=[("xnT", cc)],
                      evac="scalar" if cc % 2 else "vector")
            for cc in range(NC):
                s.op("tensor", lambda te, cc=cc: te.matmul(
                        psL[:], xnT[:, cc * 128:(cc + 1) * 128], rwT[:, cc * 64:(cc + 1) * 64],
                        start=(cc == 0), stop=(cc == NC - 1)),
                     reads=[("xnT", cc), ("rwT", cc)], writes=[("psL",)])
            s.op("vector", lambda v: v.tensor_tensor(out=lg[:], in0=psL[:], in1=rb_bc[:], op=ALU.add),
                 reads=[("psL",), ("rb_bc",)], writes=[("lg",)])
            s.op("vector", lambda v: v.reduce_max(out=ngm[:], in_=lg[:], axis=AX.X, negate=True),
                 reads=[("lg",)], writes=[("ngm",)])
            s.op("scalar", lambda sc: sc.activation(out=ex[:], in_=lg[:], func=AF.Exp,
                                                    bias=ngm[:, 0:1], accum_out=rsx[:, 0:1]),
                 reads=[("lg",), ("ngm",)], writes=[("ex",), ("rsx",)])
            s.op("vector", lambda v: v.reciprocal(out=rsx[:], in_=rsx[:]), reads=[("rsx",)], writes=[("rsxi",)])
            s.op("scalar", lambda sc: sc.activation(out=sc_sb[:], in_=ex[:], func=AF.Identity, scale=rsx[:, 0:1]),
                 reads=[("ex",), ("rsxi",)], writes=[("sc",)])
            s.op("vector", lambda v: v.tensor_copy(out=wrk[:], in_=lg[:]), reads=[("lg",)], writes=[("wrk",)])
            s.op("vector", lambda v: v.memset(tmask[:], 0.0), writes=[("tmask",)])
            for it in range(K_TOP):
                s.op("vector", lambda v: v.reduce_max(out=mx[:], in_=wrk[:], axis=AX.X),
                     reads=[("wrk",)], writes=[("mx",)])
                s.op("vector", lambda v: v.tensor_scalar(out=msk[:], in0=wrk[:], scalar1=mx[:, 0:1],
                                                         scalar2=None, op0=ALU.is_ge),
                     reads=[("wrk",), ("mx",)], writes=[("msk",)])
                s.op("vector", lambda v: v.scalar_tensor_tensor(out=wrk[:], in0=msk[:], scalar=-1000.0, in1=wrk[:],
                                                                op0=ALU.mult, op1=ALU.add),
                     reads=[("msk",), ("wrk",)], writes=[("wrk",)])
                s.op("vector", lambda v: v.tensor_tensor(out=tmask[:], in0=tmask[:], in1=msk[:], op=ALU.add),
                     reads=[("tmask",), ("msk",)], writes=[("tmask",)])
            s.op("vector", lambda v: v.tensor_tensor(out=wd32[:], in0=sc_sb[:], in1=tmask[:], op=ALU.mult),
                 reads=[("sc",), ("tmask",)], writes=[("wd32",)])
            s.op("vector", lambda v: v.reduce_sum(out=acc[:], in_=wd32[:], axis=AX.X),
                 reads=[("wd32",)], writes=[("acc",)])
            s.op("vector", lambda v: v.tensor_copy(out=wd_bf[:, 0:64], in_=wd32[:]),
                 reads=[("wd32",), ("wd_bf",)], writes=[("wd_bf",)])
            tpu.t(wdT[:], wd_bf[:], reads=[("wd_bf",)], writes=[("wdT",)], evac="vector")
            for ch in range(8):
                cb = ch % 2
                s.op("tensor", lambda te, cb=cb, ch=ch: te.matmul(
                        psF[cb][:], wdT[0:64, :], eb_bf[:, ch * 512:(ch + 1) * 512], start=True, stop=True),
                     reads=[("wdT",), ("eb_bf",)], writes=[("psF", cb)])
                s.op("vector", lambda v, cb=cb, ch=ch: v.scalar_tensor_tensor(
                        out=o_sb[:, ch * 512:(ch + 1) * 512], in0=xn32[:, ch * 512:(ch + 1) * 512],
                        scalar=acc[:, 0:1], in1=psF[cb][:], op0=ALU.mult, op1=ALU.add),
                     reads=[("xn32",), ("acc",), ("psF", cb)], writes=[("o", ch)])
            ln_prepare(s, o_sb, stats, mv, rstd, negmb, [("o", ch) for ch in range(8)], ("lnC2", t))
            ln_apply(s, lnt[:], o_sb[:], rstd, negmb, ("lnC2", t), [("o", ch) for ch in range(8)], ("lnt",))
            s.op("vector", lambda v: v.tensor_tensor(out=yt[:], in0=yt[:], in1=lnt[:], op=ALU.add),
                 reads=[("yt",), ("lnt",)], writes=[("yt",)])
            s.dma("st", lambda sy, t=t: sy.dma_start(out=y2[t * 128:(t + 1) * 128, :], in_=yt[:]),
                  reads=[("yt",)], writes=[])
        if debug:
            s.dma("dbg", lambda sy: sy.dma_start(out=dbgL[:], in_=lg[:]), reads=[("lg",)], writes=[])
            s.dma("dbg", lambda sy: sy.dma_start(out=dbgSC[:], in_=sc_sb[:]), reads=[("sc",)], writes=[])
            s.dma("dbg", lambda sy: sy.dma_start(out=dbgTM[:], in_=tmask[:]), reads=[("tmask",)], writes=[])
            s.dma("dbg", lambda sy: sy.dma_start(out=dbgAC[:, 0:1], in_=acc[:], allow_slow_non_contiguous=True), reads=[("acc",)], writes=[])
            s.dma("dbg", lambda sy: sy.dma_start(out=dbgO[:], in_=o_sb[:]), reads=[("o", ch) for ch in range(8)], writes=[])
        s.emit()
    return nc



def build_l1qkv(parts=(1,2,3,4)):
    nc = bass.Bass("TRN2", target_bir_lowering=False, debug=False)
    yd = nc.dram_tensor("y", [T, D], F32, kind="ExternalInput").ap()
    wq = nc.dram_tensor("wq", [D, 4096], F32, kind="ExternalInput").ap()
    wk = nc.dram_tensor("wk", [D, 1024], F32, kind="ExternalInput").ap()
    wv = nc.dram_tensor("wv", [D, 1024], F32, kind="ExternalInput").ap()
    pos = nc.dram_tensor("pos", [1, T], F32, kind="ExternalInput").ap()
    qTr = nc.dram_tensor("qTr", [4096, T], BF16, kind="ExternalOutput").ap()
    kTr = nc.dram_tensor("kTr", [1024, T], BF16, kind="ExternalOutput").ap()
    vtok = nc.dram_tensor("vtok", [T, 1024], BF16, kind="ExternalOutput").ap()

    wq_r = wq.rearrange("(cc p) f -> p cc f", p=128)
    wk_r = wk.rearrange("(cc p) f -> p cc f", p=128)
    wv_r = wv.rearrange("(cc p) f -> p cc f", p=128)
    qTr_r = qTr.rearrange("(f p) t -> p f t", p=128)
    kTr_r = kTr.rearrange("(f p) t -> p f t", p=128)

    s = Sched(nc)
    with ExitStack() as ctx:
        sb = lambda *a: ctx.enter_context(nc.sbuf_tensor(*a))
        ps = lambda *a: ctx.enter_context(nc.psum_tensor(*a))
        tpu = TPU(s, nc, ctx)
        xt = sb("xtD", [128, D], F32)
        stats = sb("statsD", [128, 48], F32)
        mv = sb("mvD", [128, 2], F32)
        rstd = sb("rstdD", [128, 1], F32)
        negmb = sb("negmbD", [128, 1], F32)
        xn_bf = sb("xn_bfD", [128, D], BF16)
        xnT = sb("xnTD", [128, NC * T], BF16)
        wstage = [sb(f"wstD{i}", [128, NC * 128], F32) for i in range(2)]
        wbf = [sb(f"wbfD{i}", [128, NC * 128], BF16) for i in range(2)]
        rope_bf = [sb(f"ropeD{i}", [128, T], BF16) for i in range(4)]
        t1 = sb("t1D", [64, T], F32)
        t2 = sb("t2D", [64, T], F32)
        jrow = sb("jrowD", [1, 64], F32)
        pos_sb = sb("posD", [1, T], F32)
        ang = sb("angD", [64, T], F32)
        ang2 = sb("ang2D", [64, T], F32)
        kt_ = sb("ktD", [64, T], F32)
        rt_ = sb("rtD", [64, T], F32)
        sin_sb = sb("sinD", [64, T], F32)
        cos_sb = sb("cosD", [64, T], F32)
        vT = sb("vTD", [128, 8 * T], BF16)
        vtt = sb("vttD", [128, 8 * 128], BF16)
        psq = [ps(f"psqD{i}", [128, 1024], F32) for i in range(3)]

        s.op("gpsimd", lambda g: g.iota(jrow[:], [[1, 64]], channel_multiplier=0,
                                        allow_small_or_imprecise_dtypes=True), writes=[("jrow",)])
        s.op("scalar", lambda sc: sc.activation(out=jrow[:], in_=jrow[:], func=AF.Exp,
                                                scale=-float(np.log(10000.0) / 64.0)),
             reads=[("jrow",)], writes=[("invf",)])
        s.dma("cst", lambda sy: sy.dma_start(out=pos_sb[:], in_=pos[:]), writes=[("pos",)])
        for h in range(2):
            s.op("tensor", lambda te, h=h: te.matmul(psq[0][0:64, h * 512:(h + 1) * 512], jrow[:],
                                                     pos_sb[:, h * 512:(h + 1) * 512], start=True, stop=True),
                 reads=[("invf",), ("pos",)], writes=[("psq", 0)])
        s.op("vector", lambda v: v.tensor_copy(out=ang[:], in_=psq[0][0:64, :]), reads=[("psq", 0)], writes=[("ang",)])
        s.op("vector", lambda v: v.tensor_scalar(out=ang2[:], in0=ang[:], scalar1=float(np.pi / 2),
                                                 scalar2=None, op0=ALU.add),
             reads=[("ang",)], writes=[("ang2",)])

        def wrap_sin(src_ap, srckey, out_ap, outkey):
            s.op("vector", lambda v: v.tensor_scalar(out=kt_[:], in0=src_ap, scalar1=float(1.0 / TWO_PI),
                                                     scalar2=MAGIC, op0=ALU.mult, op1=ALU.add),
                 reads=[srckey], writes=[("kt1",)])
            s.op("vector", lambda v: v.tensor_scalar(out=kt_[:], in0=kt_[:], scalar1=-MAGIC,
                                                     scalar2=None, op0=ALU.add),
                 reads=[("kt1",)], writes=[("kt2",)])
            s.op("vector", lambda v: v.scalar_tensor_tensor(out=rt_[:], in0=kt_[:], scalar=-CW_C1,
                                                            in1=src_ap, op0=ALU.mult, op1=ALU.add),
                 reads=[("kt2",), srckey], writes=[("rt1",)])
            s.op("vector", lambda v: v.scalar_tensor_tensor(out=rt_[:], in0=kt_[:], scalar=-CW_C2,
                                                            in1=rt_[:], op0=ALU.mult, op1=ALU.add),
                 reads=[("kt2",), ("rt1",)], writes=[("rt2",)])
            s.op("scalar", lambda sc: sc.activation(out=out_ap, in_=rt_[:], func=AF.Sin),
                 reads=[("rt2",)], writes=[outkey])

        if 1 in parts:
            wrap_sin(ang[:], ("ang",), sin_sb[:], ("sin",))
            wrap_sin(ang2[:], ("ang2",), cos_sb[:], ("cos",))
        else:
            s.op("vector", lambda v: v.memset(sin_sb[:], 0.1), writes=[("sin",)])
            s.op("vector", lambda v: v.memset(cos_sb[:], 0.9), writes=[("cos",)])

        for t in range(NT):
            s.dma("x", lambda sy, t=t: sy.dma_start(out=xt[:], in_=yd[t * 128:(t + 1) * 128, :]),
                  writes=[("xt",)])
            ln_prepare(s, xt, stats, mv, rstd, negmb, [("xt",)], ("lnD", t))
            ln_apply(s, xn_bf[:], xt[:], rstd, negmb, ("lnD", t), [("xt",)], ("xn_bf",))
            for cc in range(NC):
                tpu.t(xnT[:, cc * T + t * 128: cc * T + (t + 1) * 128],
                      xn_bf[:, cc * 128:(cc + 1) * 128],
                      reads=[("xn_bf",)], writes=[("xnT", t, cc)],
                      evac="scalar" if cc % 2 else "vector")
        xnT_ready = [("xnT", t, cc) for t in range(NT) for cc in range(NC)]

        chunks = [("q", f) for f in range(32)] + [("k", f) for f in range(8)] + [("v", f) for f in range(8)]
        first = True
        for ci, (kind, f) in enumerate(chunks):
            p = ci % 2
            pq = ci % 3
            pr = ci % 4
            src = {"q": wq_r, "k": wk_r, "v": wv_r}[kind]
            s.dma(f"w{p}", lambda sy, src=src, f=f, p=p: sy.dma_start(
                    out=wstage[p][:], in_=src[:, :, f * 128:(f + 1) * 128]),
                  writes=[("wst", p)])
            s.op("scalar", lambda sc, p=p: sc.copy(out=wbf[p][:], in_=wstage[p][:]),
                 reads=[("wst", p)], writes=[("wbf", p)])
            for cc in range(NC):
                for th in range(2):
                    s.op("tensor", lambda te, pq=pq, p=p, th=th, cc=cc: te.matmul(
                            psq[pq][:, th * 512:(th + 1) * 512], wbf[p][:, cc * 128:(cc + 1) * 128],
                            xnT[:, cc * T + th * 512: cc * T + (th + 1) * 512],
                            start=(cc == 0), stop=(cc == NC - 1)),
                         reads=[("wbf", p)] + (xnT_ready if first else []),
                         writes=[("psq", pq)])
                    first = False
            if kind in ("q", "k") and 3 not in parts:
                dst = qTr_r if kind == "q" else kTr_r
                s.op("scalar", lambda sc, p=p: sc.copy(out=rope_bf[p][:], in_=psq[p][:]),
                     reads=[("psq", p)], writes=[("rope", p, 0), ("rope", p, 1)])
                s.dma("st", lambda sy, dst=dst, f=f, p=p: sy.dma_start(out=dst[:, f, :], in_=rope_bf[p][:]),
                      reads=[("rope", p, 0), ("rope", p, 1)], writes=[])
            elif kind in ("q", "k"):
                s.op("vector", lambda v, pq=pq: v.tensor_tensor(out=t1[:], in0=psq[pq][0:64, :], in1=cos_sb[:], op=ALU.mult),
                     reads=[("psq", pq), ("cos",)], writes=[("t1",)])
                s.op("vector", lambda v, pq=pq: v.tensor_tensor(out=t2[:], in0=psq[pq][64:128, :], in1=sin_sb[:], op=ALU.mult),
                     reads=[("psq", pq), ("sin",)], writes=[("t2",)])
                s.op("vector", lambda v, pr=pr: v.tensor_tensor(out=rope_bf[pr][0:64, :], in0=t1[:], in1=t2[:], op=ALU.subtract),
                     reads=[("t1",), ("t2",)], writes=[("rope", pr, 0)])
                s.op("vector", lambda v, pq=pq: v.tensor_tensor(out=t1[:], in0=psq[pq][0:64, :], in1=sin_sb[:], op=ALU.mult),
                     reads=[("psq", pq), ("sin",)], writes=[("t1",)])
                s.op("vector", lambda v, pq=pq: v.tensor_tensor(out=t2[:], in0=psq[pq][64:128, :], in1=cos_sb[:], op=ALU.mult),
                     reads=[("psq", pq), ("cos",)], writes=[("t2",)])
                s.op("vector", lambda v, pr=pr: v.tensor_tensor(out=rope_bf[pr][64:128, :], in0=t1[:], in1=t2[:], op=ALU.add),
                     reads=[("t1",), ("t2",)], writes=[("rope", pr, 1)])
                dst = qTr_r if kind == "q" else kTr_r
                s.dma(f"st{pr % 2}", lambda sy, dst=dst, f=f, pr=pr: sy.dma_start(out=dst[:, f, :], in_=rope_bf[pr][:]),
                      reads=[("rope", pr, 0), ("rope", pr, 1)], writes=[], queue="gpsimd")
            else:
                dd = f
                s.op("scalar", lambda sc, pq=pq, dd=dd: sc.copy(out=vT[:, dd * T:(dd + 1) * T], in_=psq[pq][:]),
                     reads=[("psq", pq)], writes=[("vT", dd)])

        for tt in range(NT):
            for dd in range(8):
                tpu.t(vtt[:, dd * 128:(dd + 1) * 128],
                      vT[:, dd * T + tt * 128: dd * T + (tt + 1) * 128],
                      reads=[("vT", dd)], writes=[("vtt", dd)],
                      evac="scalar" if dd % 2 else "vector")
            s.dma("vt", lambda sy, tt=tt: sy.dma_start(
                    out=vtok[tt * 128:(tt + 1) * 128, :], in_=vtt[:]),
                  reads=[("vtt", dd) for dd in range(8)], writes=[("vttf",)])
        s.emit()
    return nc



def build_l1attn():
    nc = bass.Bass("TRN2", target_bir_lowering=False, debug=False)
    yd = nc.dram_tensor("y", [T, D], F32, kind="ExternalInput").ap()
    qTr = nc.dram_tensor("qTr", [4096, T], BF16, kind="ExternalInput").ap()
    kw = nc.dram_tensor("kw", [1024, 2048], BF16, kind="ExternalInput").ap()
    vw = nc.dram_tensor("vw", [2048, 1024], BF16, kind="ExternalInput").ap()
    wo = nc.dram_tensor("wo", [4096, D], F32, kind="ExternalInput").ap()
    qpos = nc.dram_tensor("qpos", [1, T], F32, kind="ExternalInput").ap()
    kpos = nc.dram_tensor("kpos", [1, 2048], F32, kind="ExternalInput").ap()
    y3 = nc.dram_tensor("y3", [T, D], F32, kind="ExternalOutput").ap()

    kw_r = kw.rearrange("(h p) k -> p h k", p=128)
    vw_r = vw.rearrange("(kt p) d -> p kt d", p=128)
    qTr_r = qTr.rearrange("(f p) t -> p f t", p=128)
    wo_r = wo.rearrange("(dd p) c -> p dd c", p=128)

    SM = float(128.0 ** -0.5)
    NSPAN = 9

    s = Sched(nc)
    with ExitStack() as ctx:
        sb = lambda *a: ctx.enter_context(nc.sbuf_tensor(*a))
        ps = lambda *a: ctx.enter_context(nc.psum_tensor(*a))
        oT_all = sb("oT_all", [128, NC * T], BF16)
        psS = [ps("psSE0", [128, 1536], F32)]
        psO = [ps("psOE0", [128, 128], F32)]

        with ExitStack() as actx:
            asb = lambda *a: actx.enter_context(nc.sbuf_tensor(*a))
            tpu = TPU(s, nc, actx, nbanks=3)
            kwin = asb("kwinE", [128, 8, 2048], BF16)
            vwin = asb("vwinE", [128, 16, 1024], BF16)
            qt_sb = [asb(f"qtE{i}", [128, 32 * 128], BF16) for i in range(2)]
            o_bf = asb("o_bfE", [128, 4096], BF16)
            p_sb = [asb(f"p_sbE{i}", [128, 1152], BF16) for i in range(3)]
            Ssb = [asb(f"SsbE{i}", [128, 1152], F32) for i in range(3)]
            pT_sb = [asb(f"pT_sbE{i}", [128, 1152], BF16) for i in range(3)]
            kposb = asb("kposbE", [128, 2048], F32)
            qpos_sb = asb("qposE", [128, 8], F32)
            qpm_sb = asb("qpmE", [128, 8], F32)
            kpr = asb("kprE", [1, 2048], F32)
            ones1 = asb("ones1E", [1, 128], F32)
            masks = asb("masksE", [128, NSPAN * 128], F32)
            m1_sb = asb("m1E", [128, 128], F32)
            ngx = asb("ngxE", [128, 3], F32)
            rs_sb = asb("rsE", [128, 3], F32)
            rsi_sb = asb("rsiE", [128, 3], F32)

            s.op("vector", lambda v: v.memset(ones1[:], 1.0), writes=[("ones1",)])
            s.dma("cst", lambda sy: sy.dma_start(out=kpr[:], in_=kpos[:]), writes=[("kpr",)])
            s.dma("cst", lambda sy: sy.dma_start(out=kwin[:], in_=kw_r[:]), writes=[("kwin",)])
            s.dma("cst", lambda sy: sy.dma_start(out=vwin[:], in_=vw_r[:]), writes=[("vwin",)])
            for qq in range(8):
                s.dma("cst", lambda sy, qq=qq: sy.dma_start(
                        out=qpos_sb[:, qq:qq + 1], in_=qpos[0, qq * 128:(qq + 1) * 128].unsqueeze(-1)),
                      writes=[("qpos", qq)])
            s.op("vector", lambda v: v.tensor_scalar(out=qpm_sb[:], in0=qpos_sb[:], scalar1=-1024.0,
                                                     scalar2=None, op0=ALU.add),
                 reads=[("qpos", qq) for qq in range(8)], writes=[("qpm",)])
            for h4 in range(3):
                s.op("tensor", lambda te, h4=h4: te.matmul(
                        psS[0][:, h4 * 512:(h4 + 1) * 512], ones1[:], kpr[:, h4 * 512:(h4 + 1) * 512],
                        start=True, stop=True),
                     reads=[("ones1",), ("kpr",)], writes=[("psS", 0)])
            s.op("vector", lambda v: v.tensor_copy(out=kposb[:, 0:1536], in_=psS[0][:]),
                 reads=[("psS", 0)], writes=[("kposb",)])
            s.op("tensor", lambda te: te.matmul(psS[0][:, 0:512], ones1[:], kpr[:, 1536:2048],
                                                start=True, stop=True),
                 reads=[("ones1",), ("kpr",)], writes=[("psS", 0)])
            s.op("vector", lambda v: v.tensor_copy(out=kposb[:, 1536:2048], in_=psS[0][:, 0:512]),
                 reads=[("psS", 0)], writes=[("kposb",)])

            pend = []
            pend_tp = []

            def emit_av(idx, qq, h, ib):
                kvh = h // 4
                for kt in range(NSPAN):
                    s.op("tensor", lambda te, ib=ib, qq=qq, kvh=kvh, kt=kt: te.matmul(
                            psO[0][:], pT_sb[ib][:, kt * 128:(kt + 1) * 128],
                            vwin[:, qq + kt, kvh * 128:(kvh + 1) * 128],
                            start=(kt == 0), stop=(kt == NSPAN - 1)),
                         reads=[("pT", ib), ("vwin",)], writes=[("psO", 0)])
                s.op("vector", lambda v, ib=ib: v.reciprocal(out=rsi_sb[:, ib:ib + 1], in_=rs_sb[:, ib:ib + 1]),
                     reads=[("rs", ib)], writes=[("rsi", ib)])
                s.op("scalar", lambda sc, ib=ib, h=h: sc.activation(
                        out=o_bf[:, h * 128:(h + 1) * 128], in_=psO[0][:], func=AF.Identity,
                        scale=rsi_sb[:, ib:ib + 1]),
                     reads=[("psO", 0), ("rsi", ib)], writes=[("o", h)])

            def emit_tp(idx, ib):
                for kt in range(NSPAN):
                    tpu.t(pT_sb[ib][:, kt * 128:(kt + 1) * 128],
                          p_sb[ib][:, kt * 128:(kt + 1) * 128],
                          reads=[("p", ib)], writes=[("pT", ib)],
                          evac="vector" if kt % 2 else "scalar")

            for qq in range(8):
                qb = qq % 2
                s.dma("qt", lambda sy, qq=qq, qb=qb: sy.dma_start(
                        out=qt_sb[qb][:].rearrange("p (f t) -> p f t", f=32), in_=qTr_r[:, :, qq * 128:(qq + 1) * 128]),
                      writes=[("qt", qb)])
                for kt in range(NSPAN):
                    j0 = (qq + kt) * 128
                    s.op("vector", lambda v, qq=qq, j0=j0: v.tensor_scalar(
                            out=m1_sb[:], in0=kposb[:, j0:j0 + 128], scalar1=qpos_sb[:, qq:qq + 1],
                            scalar2=None, op0=ALU.is_le),
                         reads=[("kposb",), ("qpos", qq)], writes=[("m1",)])
                    s.op("vector", lambda v, qq=qq, j0=j0, kt=kt: v.scalar_tensor_tensor(
                            out=m1_sb[:], in0=kposb[:, j0:j0 + 128],
                            scalar=qpm_sb[:, qq:qq + 1], in1=m1_sb[:], op0=ALU.is_gt, op1=ALU.mult),
                         reads=[("kposb",), ("qpm",), ("m1",)], writes=[("m1",)])
                    s.op("vector", lambda v, kt=kt: v.tensor_scalar(
                            out=masks[:, kt * 128:(kt + 1) * 128], in0=m1_sb[:],
                            scalar1=400000.0, scalar2=-400000.0, op0=ALU.mult, op1=ALU.add),
                         reads=[("m1",)], writes=[("mask", kt)])
                for h in range(32):
                    idx = qq * 32 + h
                    ib = idx % 3
                    sb2 = idx % 2
                    kvh = h // 4
                    for sp in range(3):
                        w0 = qq * 128 + sp * 512
                        wlen = min(1152, (sp + 1) * 512) - sp * 512
                        s.op("tensor", lambda te, qb=qb, h=h, kvh=kvh, w0=w0, sp=sp, wlen=wlen: te.matmul(
                                psS[0][:, sp * 512: sp * 512 + wlen], qt_sb[qb][:, h * 128:(h + 1) * 128],
                                kwin[:, kvh, w0:w0 + wlen], start=True, stop=True),
                             reads=[("qt", qb), ("kwin",)], writes=[("psS", 0)])
                    s.op("vector", lambda v, ib=ib: v.tensor_tensor(
                            out=Ssb[ib][:], in0=psS[0][:, 0:1152], in1=masks[:], op=ALU.add),
                         reads=[("psS", 0)] + [("mask", kt) for kt in range(NSPAN)],
                         writes=[("Ss", ib)])
                    s.op("vector", lambda v, ib=ib: v.reduce_max(out=ngx[:, ib:ib + 1], in_=Ssb[ib][:],
                                                                 axis=AX.X, negate=True),
                         reads=[("Ss", ib)], writes=[("ngx", ib)])
                    s.op("vector", lambda v, ib=ib: v.tensor_scalar(out=ngx[:, ib:ib + 1], in0=ngx[:, ib:ib + 1],
                                                                    scalar1=SM, scalar2=None, op0=ALU.mult),
                         reads=[("ngx", ib)], writes=[("ngx", ib)])
                    s.op("scalar", lambda sc, ib=ib: sc.activation(
                            out=p_sb[ib][:], in_=Ssb[ib][:], func=AF.Exp, bias=ngx[:, ib:ib + 1], scale=SM,
                            accum_out=rs_sb[:, ib:ib + 1]),
                         reads=[("Ss", ib), ("ngx", ib)], writes=[("p", ib), ("rs", ib)])
                    pend_tp.append((idx, ib))
                    if len(pend_tp) > 1:
                        emit_tp(*pend_tp.pop(0))
                    pend.append((idx, qq, h, ib))
                    if len(pend) > 2:
                        emit_av(*pend.pop(0))
                while pend_tp:
                    emit_tp(*pend_tp.pop(0))
                while pend:
                    emit_av(*pend.pop(0))
                for dd in range(NC):
                    tpu.t(oT_all[:, dd * T + qq * 128: dd * T + (qq + 1) * 128],
                          o_bf[:, dd * 128:(dd + 1) * 128],
                          reads=[("o", dd)], writes=[("oT", qq, dd)],
                          evac="vector" if dd % 2 else "scalar")

        s.barrier()
        with ExitStack() as mctx:
            msb = lambda *a: mctx.enter_context(nc.sbuf_tensor(*a))
            A_all = msb("A_allE", [128, 8, 4096], BF16)
            ectx = mctx.enter_context(ExitStack())
            esb = lambda *a: ectx.enter_context(nc.sbuf_tensor(*a))
            wst = [esb(f"wstE{i}", [128, 8 * 512], F32) for i in range(2)]
            wo_bf = esb("wo_bfE", [128, 32 * 512], BF16)
            oT_ready = [("oT", qq, dd) for qq in range(8) for dd in range(NC)]
            for ch in range(8):
                for hf in range(4):
                    hp = hf % 2
                    s.dma(f"w{hp}", lambda sy, ch=ch, hf=hf, hp=hp: sy.dma_start(
                            out=wst[hp][:].rearrange("p (a b) -> p a b", a=8),
                            in_=wo_r[:, hf * 8:(hf + 1) * 8, ch * 512:(ch + 1) * 512]),
                          writes=[("wst", hp)])
                    s.op("scalar", lambda sc, hf=hf, hp=hp: sc.copy(
                            out=wo_bf[:, hf * 4096:(hf + 1) * 4096], in_=wst[hp][:]),
                         reads=[("wst", hp)], writes=[("wo_bf", hf)])
                for qq in range(8):
                    ab = qq % 2
                    for dd in range(NC):
                        s.op("tensor", lambda te, ab=ab, ch=ch, qq=qq, dd=dd: te.matmul(
                                psS[0][:, ab * 512:(ab + 1) * 512],
                                oT_all[:, dd * T + qq * 128: dd * T + (qq + 1) * 128],
                                wo_bf[:, dd * 512:(dd + 1) * 512],
                                start=(dd == 0), stop=(dd == NC - 1)),
                             reads=[("wo_bf", dd // 8)] + (oT_ready if (ch == 0 and qq == 0 and dd == 0) else []),
                             writes=[("psSo", ab)])
                    s.op("scalar", lambda sc, ab=ab, ch=ch, qq=qq: sc.copy(
                            out=A_all[:, qq, ch * 512:(ch + 1) * 512], in_=psS[0][:, ab * 512:(ab + 1) * 512]),
                         reads=[("psSo", ab)], writes=[("A", qq, ch)])
            ectx.close()
            s.barrier()
            fctx = mctx.enter_context(ExitStack())
            fsb = lambda *a: fctx.enter_context(nc.sbuf_tensor(*a))
            yt = fsb("ytE", [128, D], F32)
            lnt = fsb("lntE", [128, D], F32)
            stats = fsb("statsE", [128, 48], F32)
            mv = fsb("mvE", [128, 2], F32)
            rstd = fsb("rstdE", [128, 1], F32)
            negmb = fsb("negmbE", [128, 1], F32)
            for qq in range(8):
                s.dma("x", lambda sy, qq=qq: sy.dma_start(out=yt[:], in_=yd[qq * 128:(qq + 1) * 128, :]),
                      writes=[("yt",)])
                ln_prepare(s, A_all[:, qq, :], stats, mv, rstd, negmb,
                           [("A", qq, ch) for ch in range(8)], ("lnE", qq), ncols=D, col0=0)
                ln_apply(s, lnt[:], A_all[:, qq, :], rstd, negmb, ("lnE", qq),
                         [("A", qq, ch) for ch in range(8)], ("lnt",))
                s.op("vector", lambda v: v.tensor_tensor(out=yt[:], in0=yt[:], in1=lnt[:], op=ALU.add),
                     reads=[("yt",), ("lnt",)], writes=[("yt",)])
                s.dma("st", lambda sy, qq=qq: sy.dma_start(out=y3[qq * 128:(qq + 1) * 128, :], in_=yt[:]),
                      reads=[("yt",)], writes=[])
        s.emit()
    return nc



def build_l1ffn():
    nc = bass.Bass("TRN2", target_bir_lowering=False, debug=False)
    yd = nc.dram_tensor("y", [T, D], F32, kind="ExternalInput").ap()
    w8 = nc.dram_tensor("w8", [D, D], I8, kind="ExternalInput").ap()
    scl = nc.dram_tensor("scl", [1, D], F32, kind="ExternalInput").ap()
    bias = nc.dram_tensor("bias", [D], F32, kind="ExternalInput").ap()
    yo = nc.dram_tensor("yo", [T, D], F32, kind="ExternalOutput").ap()

    w8_r = w8.rearrange("(oc p) i -> p oc i", p=128)

    s = Sched(nc)
    with ExitStack() as ctx:
        sb = lambda *a: ctx.enter_context(nc.sbuf_tensor(*a))
        ps = lambda *a: ctx.enter_context(nc.psum_tensor(*a))
        AT_all = sb("AT_allF", [128, NC * T], BF16)
        psF = [ps(f"psFF{i}", [128, 512], F32) for i in range(2)]
        scl_sb = sb("sclF", [128, 32], F32)
        bias_sb = sb("biasF", [128, 32], F32)

        with ExitStack() as actx:
            asb = lambda *a: actx.enter_context(nc.sbuf_tensor(*a))
            tpu = TPU(s, nc, actx)
            xt = asb("xtF", [128, D], F32)
            stats = asb("statsF", [128, 48], F32)
            mv = asb("mvF", [128, 2], F32)
            rstd = asb("rstdF", [128, 1], F32)
            negmb = asb("negmbF", [128, 1], F32)
            xn_bf = [asb(f"xn_bfF{i}", [128, D], BF16) for i in range(2)]
            xnT = asb("xnTF", [128, NC * T], BF16)
            w8st = [asb(f"w8stF{i}", [128, D], I8) for i in range(2)]
            w8bf = [asb(f"w8bfF{i}", [128, D], BF16) for i in range(2)]
            w8T = [asb(f"w8TF{i}", [128, NC * 128], BF16) for i in range(2)]

            for oc in range(32):
                s.dma("sc", lambda sy, oc=oc: sy.dma_start(
                        out=scl_sb[:, oc:oc + 1], in_=scl[0, oc * 128:(oc + 1) * 128].unsqueeze(-1)),
                      writes=[("scl", oc)])
                s.dma("sc", lambda sy, oc=oc: sy.dma_start(
                        out=bias_sb[:, oc:oc + 1], in_=bias[oc * 128:(oc + 1) * 128].unsqueeze(-1)),
                      writes=[("bias", oc)])

            for t in range(NT):
                tp = t % 2
                s.dma("x", lambda sy, t=t: sy.dma_start(out=xt[:], in_=yd[t * 128:(t + 1) * 128, :]),
                      writes=[("xt",)])
                ln_prepare(s, xt, stats, mv, rstd, negmb, [("xt",)], ("lnF", t))
                ln_apply(s, xn_bf[tp][:], xt[:], rstd, negmb, ("lnF", t), [("xt",)], ("xn_bf", tp))
                for cc in range(NC):
                    tpu.t(xnT[:, cc * T + t * 128: cc * T + (t + 1) * 128],
                          xn_bf[tp][:, cc * 128:(cc + 1) * 128],
                          reads=[("xn_bf", tp)], writes=[("xnT", t, cc)], evac="vector")
            xnT_ready = [("xnT", t, cc) for t in range(NT) for cc in range(NC)]

            first = True
            for oc in range(32):
                p = oc % 2
                s.dma(f"w{p}", lambda sy, oc=oc, p=p: sy.dma_start(out=w8st[p][:], in_=w8[oc * 128:(oc + 1) * 128, :]),
                      writes=[("w8st", p)])
                s.op("scalar", lambda sc, p=p: sc.copy(out=w8bf[p][:], in_=w8st[p][:]),
                     reads=[("w8st", p)], writes=[("w8bf", p)])
                for ic in range(NC):
                    tpu.t(w8T[p][:, ic * 128:(ic + 1) * 128], w8bf[p][:, ic * 128:(ic + 1) * 128],
                          reads=[("w8bf", p)], writes=[("w8T", p)], evac="vector")
                for ic in range(NC):
                    for th in range(2):
                        s.op("tensor", lambda te, p=p, th=th, ic=ic: te.matmul(
                                psF[th][:], w8T[p][:, ic * 128:(ic + 1) * 128],
                                xnT[:, ic * T + th * 512: ic * T + (th + 1) * 512],
                                start=(ic == 0), stop=(ic == NC - 1)),
                             reads=[("w8T", p)] + (xnT_ready if first else []),
                             writes=[("psF", th)], tag="mmF")
                        first = False
                for th in range(2):
                    s.op("scalar", lambda sc, oc=oc, th=th: sc.activation(
                            out=AT_all[:, oc * T + th * 512: oc * T + (th + 1) * 512], in_=psF[th][:],
                            func=AF.Identity, bias=bias_sb[:, oc:oc + 1], scale=scl_sb[:, oc:oc + 1]),
                         reads=[("psF", th), ("scl", oc), ("bias", oc)], writes=[("AT", oc, th)])

        s.barrier()
        mctx = ctx.enter_context(ExitStack())
        A_tok = mctx.enter_context(nc.sbuf_tensor("A_tokF", [128, 8, 4096], BF16))
        tpu2 = TPU(s, nc, mctx)
        for oc in range(32):
            for tt in range(NT):
                tpu2.t(A_tok[:, tt, oc * 128:(oc + 1) * 128],
                       AT_all[:, oc * T + tt * 128: oc * T + (tt + 1) * 128],
                       reads=[("AT", oc, 0), ("AT", oc, 1)], writes=[("A_tok", tt, oc)],
                       evac="scalar" if (oc + tt) % 2 else "vector")

        s.barrier()
        with ExitStack() as fctx:
            fsb = lambda *a: fctx.enter_context(nc.sbuf_tensor(*a))
            yt = fsb("ytF2", [128, D], F32)
            lnt = fsb("lntF2", [128, D], F32)
            stats = fsb("statsF2", [128, 48], F32)
            mv = fsb("mvF2", [128, 2], F32)
            rstd = fsb("rstdF2", [128, 1], F32)
            negmb = fsb("negmbF2", [128, 1], F32)
            for tt in range(NT):
                s.dma("x2", lambda sy, tt=tt: sy.dma_start(out=yt[:], in_=yd[tt * 128:(tt + 1) * 128, :]),
                      writes=[("yt",)])
                ln_prepare(s, A_tok[:, tt, :], stats, mv, rstd, negmb,
                           [("A_tok", tt, oc) for oc in range(32)], ("lnF2", tt))
                ln_apply(s, lnt[:], A_tok[:, tt, :], rstd, negmb, ("lnF2", tt),
                         [("A_tok", tt, oc) for oc in range(32)], ("lnt",))
                s.op("vector", lambda v: v.tensor_tensor(out=yt[:], in0=yt[:], in1=lnt[:], op=ALU.add),
                     reads=[("yt",), ("lnt",)], writes=[("yt",)])
                s.dma("st", lambda sy, tt=tt: sy.dma_start(out=yo[tt * 128:(tt + 1) * 128, :], in_=yt[:]),
                      reads=[("yt",)], writes=[])
        s.emit()
    return nc


_BUILT = {}


def _get(name, builder):
    if name not in _BUILT:
        _BUILT[name] = builder()
    return _BUILT[name]


def _run_l0qkv(x_full, inp):
    ncA = _get("l0qkv", build_l0qkv)
    wqkv = np.ascontiguousarray(inp["L0_attn_Wqkv"])
    bqkv = np.ascontiguousarray(inp["L0_attn_bqkv"])
    in_maps = [dict(x=np.ascontiguousarray(x_full[c * T:(c + 1) * T]), wqkv=wqkv, bqkv=bqkv)
               for c in range(NCORES)]
    res = run_bass_kernel_spmd(ncA, in_maps, core_ids=list(range(NCORES)))
    qT = [np.asarray(r["qT"]) for r in res.results]
    kT_full = np.ascontiguousarray(np.concatenate([np.asarray(r["kT"]) for r in res.results], axis=1))
    v_full = np.ascontiguousarray(np.concatenate([np.asarray(r["vtok"]) for r in res.results], axis=0))
    return qT, kT_full, v_full


def _run_l0attn(x_full, qT, kT_full, v_full, inp):
    ncB = _get("l0attn", build_l0attn)
    wo = np.ascontiguousarray(inp["L0_attn_Wo"])
    bo = np.ascontiguousarray(inp["L0_attn_bo"])
    in_maps = [dict(x=np.ascontiguousarray(x_full[c * T:(c + 1) * T]), qT=np.ascontiguousarray(qT[c]),
                    kTf=kT_full, vf=v_full, wo=wo, bo=bo)
               for c in range(NCORES)]
    res = run_bass_kernel_spmd(ncB, in_maps, core_ids=list(range(NCORES)))
    return np.concatenate([np.asarray(r["y"]) for r in res.results], axis=0)


def _run_l0ffn(y1, inp):
    ncC = _get("l0ffn", build_l0ffn)
    rw = np.ascontiguousarray(inp["L0_ffn_router_weight"])
    rb = np.ascontiguousarray(inp["L0_ffn_router_bias"])
    eb = np.ascontiguousarray(inp["L0_ffn_expert_bias"])
    in_maps = [dict(y=np.ascontiguousarray(y1[c * T:(c + 1) * T]), rw=rw, rb=rb, eb=eb)
               for c in range(NCORES)]
    res = run_bass_kernel_spmd(ncC, in_maps, core_ids=list(range(NCORES)))
    return np.concatenate([np.asarray(r["y2"]) for r in res.results], axis=0)


def _run_l1qkv(y2, inp):
    ncD = _get("l1qkv", build_l1qkv)
    wq = np.ascontiguousarray(inp["L1_attn_wq"])
    wk = np.ascontiguousarray(inp["L1_attn_wk"])
    wv = np.ascontiguousarray(inp["L1_attn_wv"])
    pos = np.arange(8192, dtype=np.float32)
    in_maps = [dict(y=np.ascontiguousarray(y2[c * T:(c + 1) * T]), wq=wq, wk=wk, wv=wv,
                    pos=np.ascontiguousarray(pos[c * T:(c + 1) * T].reshape(1, T)))
               for c in range(NCORES)]
    res = run_bass_kernel_spmd(ncD, in_maps, core_ids=list(range(NCORES)))
    qTr = [np.asarray(r["qTr"]) for r in res.results]
    kTr_full = np.ascontiguousarray(np.concatenate([np.asarray(r["kTr"]) for r in res.results], axis=1))
    v_full = np.ascontiguousarray(np.concatenate([np.asarray(r["vtok"]) for r in res.results], axis=0))
    return qTr, kTr_full, v_full


def _run_l1attn(y2, qTr, kTr_full, v_full, inp):
    ncE = _get("l1attn", build_l1attn)
    wo = np.ascontiguousarray(inp["L1_attn_wo"])
    pos = np.arange(8192, dtype=np.float32)
    in_maps = []
    for c in range(NCORES):
        if c == 0:
            kw = np.concatenate([kTr_full[:, -T:], kTr_full[:, :T]], axis=1)
            vw = np.concatenate([v_full[-T:, :], v_full[:T, :]], axis=0)
            kpos = np.concatenate([np.full(T, -1e6, np.float32), pos[:T]])
        else:
            kw = kTr_full[:, (c - 1) * T:(c + 1) * T]
            vw = v_full[(c - 1) * T:(c + 1) * T, :]
            kpos = pos[(c - 1) * T:(c + 1) * T]
        in_maps.append(dict(y=np.ascontiguousarray(y2[c * T:(c + 1) * T]),
                            qTr=np.ascontiguousarray(qTr[c]),
                            kw=np.ascontiguousarray(kw), vw=np.ascontiguousarray(vw), wo=wo,
                            qpos=np.ascontiguousarray(pos[c * T:(c + 1) * T].reshape(1, T)),
                            kpos=np.ascontiguousarray(kpos.reshape(1, 2 * T))))
    res = run_bass_kernel_spmd(ncE, in_maps, core_ids=list(range(NCORES)))
    return np.concatenate([np.asarray(r["y3"]) for r in res.results], axis=0)


def _run_l1ffn(y3, inp):
    ncF = _get("l1ffn", build_l1ffn)
    w8 = np.ascontiguousarray(inp["L1_ffn_weight_q"])
    scl = np.ascontiguousarray(inp["L1_ffn_scale"])
    bias = np.ascontiguousarray(inp["L1_ffn_bias"])
    in_maps = [dict(y=np.ascontiguousarray(y3[c * T:(c + 1) * T]), w8=w8, scl=scl, bias=bias)
               for c in range(NCORES)]
    res = run_bass_kernel_spmd(ncF, in_maps, core_ids=list(range(NCORES)))
    return np.concatenate([np.asarray(r["yo"]) for r in res.results], axis=0)


def kernel(**inputs):
    inp = {k: np.asarray(v) for k, v in inputs.items()}
    x_full = np.ascontiguousarray(inp["L0_attn_x"].reshape(8192, D))

    qT, kT_full, v_full = _run_l0qkv(x_full, inp)
    y1 = _run_l0attn(x_full, qT, kT_full, v_full, inp)
    y2 = _run_l0ffn(y1, inp)
    qTr, kTr_full, vr_full = _run_l1qkv(y2, inp)
    y3 = _run_l1attn(y2, qTr, kTr_full, vr_full, inp)
    yo = _run_l1ffn(y3, inp)
    return yo.reshape(1, 8192, D).astype(np.float32)


# version 15
# speedup vs baseline: 1.1742x; 1.1742x over previous
import numpy as np
import ml_dtypes
import concourse.bass as bass
import concourse.mybir as mybir
from concourse.bass_utils import run_bass_kernel_spmd
from contextlib import ExitStack

F32 = mybir.dt.float32
BF16 = mybir.dt.bfloat16
I8 = mybir.dt.int8
AF = mybir.ActivationFunctionType
ALU = mybir.AluOpType
AX = mybir.AxisListType

NCORES = 8
T = 1024
D = 4096
NT = T // 128
NC = D // 128

MAGIC = 12582912.0
TWO_PI = 6.283185307179586
CW_C1 = 6.28125
CW_C2 = 0.0019353071795864769


COMPUTE = ("tensor", "vector", "scalar", "gpsimd")
QUEUES = ("sync", "scalar", "gpsimd")


class Sched:
    def __init__(self, nc):
        self.nc = nc
        self.q = {e: [] for e in ("sync", "tensor", "vector", "scalar", "gpsimd")}
        self.cnt = {e: 0 for e in COMPUTE}
        self.dcnt = {}
        self.streams = []
        self.lw = {}
        self.rd = {}
        self.seen = {}

    def barrier(self):
        snap = {("c", e): self.cnt[e] for e in COMPUTE if self.cnt[e]}
        snap.update({("d", st): 16 * self.dcnt[st] for st in self.streams})
        self.bar = snap

    def _waits(self, ckey, reads, writes):
        seen = self.seen.setdefault(ckey, {})
        waits = {}

        def need(dep):
            if dep is None:
                return
            kind, who, dc = dep
            if kind == "c":
                if who == ckey == "tensor":
                    return
                val = dc
                key = ("c", who)
            else:
                val = 16 * self.dcnt[who]
                key = ("d", who)
            if seen.get(key, 0) >= val:
                return
            waits[key] = max(waits.get(key, 0), val)

        for key, val in getattr(self, "bar", {}).items():
            if seen.get(key, 0) < val:
                waits[key] = max(waits.get(key, 0), val)
        for r in reads:
            need(self.lw.get(r))
        for w in writes:
            need(self.lw.get(w))
            for pk, dc in self.rd.get(w, {}).items():
                need((pk[0], pk[1], dc))
        for k, v in waits.items():
            seen[k] = v
        return waits

    def _record(self, pkey, cnt, reads, writes):
        for r in reads:
            self.rd.setdefault(r, {})[pkey] = cnt
        for w in writes:
            self.lw[w] = (pkey[0], pkey[1], cnt)
            self.rd[w] = {}

    def op(self, eng, fn, reads=(), writes=(), tag=None):
        waits = self._waits(eng, reads, writes)
        self.cnt[eng] += 1
        self._record(("c", eng), self.cnt[eng], reads, writes)
        self.q[eng].append((waits, fn, ("c", eng), 1, (tag, tuple(reads), tuple(writes), fn.__code__.co_firstlineno)))

    def dma(self, stream, fn, reads=(), writes=(), queue="sync", tag=None):
        if stream not in self.dcnt:
            self.dcnt[stream] = 0
            self.streams.append(stream)
        waits = self._waits(queue if queue != "sync" else "syncq", reads, writes)
        self.dcnt[stream] += 1
        self._record(("d", stream), self.dcnt[stream], reads, writes)
        self.q[queue].append((waits, fn, ("d", stream), 16, (tag, tuple(reads), tuple(writes), fn.__code__.co_firstlineno)))

    def emit(self):
        nc = self.nc
        with ExitStack() as ctx:
            sems = {}
            for e in COMPUTE:
                sems[("c", e)] = ctx.enter_context(nc.semaphore(f"sem_{e}"))
            for st in self.streams:
                sems[("d", st)] = ctx.enter_context(nc.semaphore(f"sem_{st}"))
            with nc.Block() as blk:
                def body(e):
                    def _(engobj):
                        for waits, fn, pkey, amt, tag in self.q[e]:
                            for k, v in waits.items():
                                engobj.wait_ge(sems[k], v)
                            fn(engobj).then_inc(sems[pkey], amt)
                        if e == "sync":
                            for st in self.streams:
                                engobj.wait_ge(sems[("d", st)], 16 * self.dcnt[st])
                    _.__name__ = f"body_{e}"
                    return _
                for e in ("sync", "tensor", "vector", "scalar", "gpsimd"):
                    if self.q[e] or e == "sync":
                        getattr(blk, e)(body(e))



def ln_prepare(s, sx_ap, stats, mv, rstd, negmb, readkeys, k, ncols=D, col0=0):
    nchunks = (ncols + 511) // 512
    for c in range(nchunks):
        c0, c1 = col0 + c * 512, col0 + min(ncols, (c + 1) * 512)
        s.op("vector", lambda v, c=c, c0=c0, c1=c1: v.bn_stats(stats[:, c * 6:(c + 1) * 6], sx_ap[:, c0:c1]),
             reads=list(readkeys), writes=[(k, "st", c)])
    s.op("vector", lambda v: v.bn_aggr(mv[:], stats[:, 0:6 * nchunks]),
         reads=[(k, "st", c) for c in range(nchunks)], writes=[(k, "mv")])
    s.op("vector", lambda v: v.tensor_scalar(out=rstd[:], in0=mv[:, 1:2], scalar1=1e-5, scalar2=None, op0=ALU.add),
         reads=[(k, "mv")], writes=[(k, "veps")])
    s.op("scalar", lambda sc: sc.activation(out=rstd[:], in_=rstd[:], func=AF.Sqrt),
         reads=[(k, "veps")], writes=[(k, "sd")])
    s.op("vector", lambda v: v.reciprocal(out=rstd[:], in_=rstd[:]),
         reads=[(k, "sd")], writes=[(k, "rstd")])
    s.op("vector", lambda v: v.tensor_tensor(out=negmb[:], in0=mv[:, 0:1], in1=rstd[:], op=ALU.mult),
         reads=[(k, "mv"), (k, "rstd")], writes=[(k, "mb")])
    s.op("vector", lambda v: v.tensor_scalar(out=negmb[:], in0=negmb[:], scalar1=-1.0, scalar2=None, op0=ALU.mult),
         reads=[(k, "mb")], writes=[(k, "negmb")])


def ln_apply(s, out_ap, in_ap, rstd, negmb, k, readkeys, outkey):
    s.op("scalar", lambda sc: sc.activation(out=out_ap, in_=in_ap, func=AF.Identity,
                                            bias=negmb[:, 0:1], scale=rstd[:, 0:1]),
         reads=list(readkeys) + [(k, "rstd"), (k, "negmb")], writes=[outkey])


class TPU:
    _n = 0

    def __init__(self, s, nc, ctx, nbanks=2):
        TPU._n += 1
        self.name = f"tpu{TPU._n}"
        self.ident = ctx.enter_context(nc.sbuf_tensor(f"{self.name}_id", [128, 128], BF16))
        self.colm = ctx.enter_context(nc.sbuf_tensor(f"{self.name}_cm", [128, 128], F32))
        self.rowi = ctx.enter_context(nc.sbuf_tensor(f"{self.name}_ri", [128, 1], F32))
        self.nbanks = nbanks
        self.ps = [ctx.enter_context(nc.psum_tensor(f"{self.name}_ps{i}", [128, 1024], BF16))
                   for i in range(nbanks)]
        self.n = 0
        self.s = s
        k = self.name
        s.op("gpsimd", lambda g: g.iota(self.colm[:], [[1, 128]], channel_multiplier=0,
                                        allow_small_or_imprecise_dtypes=True), writes=[(k, "cm")])
        s.op("gpsimd", lambda g: g.iota(self.rowi[:], [[0, 1]], channel_multiplier=1,
                                        allow_small_or_imprecise_dtypes=True), writes=[(k, "ri")])
        s.op("vector", lambda v: v.tensor_scalar(out=self.ident[:], in0=self.colm[:],
                                                 scalar1=self.rowi[:, 0:1], scalar2=None, op0=ALU.is_equal),
             reads=[(k, "cm"), (k, "ri")], writes=[(k, "id")])

    def t(self, out_ap, in_ap, reads, writes, evac="vector", p=None):
        s, k = self.s, self.name
        b = self.n % self.nbanks
        sl = (self.n // self.nbanks) % 8
        self.n += 1
        pdim = 128 if p is None else p
        ident = self.ident[:] if pdim == 128 else self.ident[0:pdim, 0:pdim]
        dst = self.ps[b][:, sl * 128:sl * 128 + pdim]
        s.op("tensor", lambda te, dst=dst: te.transpose(dst, in_ap, ident),
             reads=list(reads) + [(k, "id")], writes=[(k, "s", b, sl), (k, "bk", b)])
        if evac == "vector":
            s.op("vector", lambda e, dst=dst: e.tensor_copy(out=out_ap, in_=dst),
                 reads=[(k, "s", b, sl), (k, "bk", b)], writes=list(writes))
        else:
            s.op("scalar", lambda e, dst=dst: e.copy(out=out_ap, in_=dst),
                 reads=[(k, "s", b, sl), (k, "bk", b)], writes=list(writes))



def build_l0qkv():
    standalone = True
    nc = bass.Bass("TRN2", target_bir_lowering=False, debug=False)
    x = nc.dram_tensor("x", [T, D], F32, kind="ExternalInput").ap()
    wqkv = nc.dram_tensor("wqkv", [D, 3072], F32, kind="ExternalInput").ap()
    bqkv = nc.dram_tensor("bqkv", [3072], F32, kind="ExternalInput").ap()
    qT = nc.dram_tensor("qT", [1024, T], BF16, kind="ExternalOutput").ap()
    kT = nc.dram_tensor("kT", [1024, T], BF16, kind="ExternalOutput").ap()
    vtok = nc.dram_tensor("vtok", [T, 1024], BF16, kind="ExternalOutput").ap()

    wq_r = wqkv.rearrange("(cc p) f -> p cc f", p=128)
    qT_r = qT.rearrange("(f p) t -> p f t", p=128)
    kT_r = kT.rearrange("(f p) t -> p f t", p=128)

    s = Sched(nc)
    with ExitStack() as ctx:
        sb = lambda *a: ctx.enter_context(nc.sbuf_tensor(*a))
        ps = lambda *a: ctx.enter_context(nc.psum_tensor(*a))
        tpu = TPU(s, nc, ctx)
        xt = sb("xt", [128, D], F32)
        stats = sb("stats", [128, 48], F32)
        mv = sb("mv", [128, 2], F32)
        rstd = sb("rstd", [128, 1], F32)
        negmb = sb("negmb", [128, 1], F32)
        xn_bf2 = [sb(f"xn_bf{i}", [128, D], BF16) for i in range(2)]
        xnT = sb("xnT", [128, NC * T], BF16)
        bq = sb("bq", [128, 24], F32)
        wstage = [sb(f"wstage{i}", [128, NC * 128], F32) for i in range(2)]
        wbf = [sb(f"wbf{i}", [128, NC * 128], BF16) for i in range(2)]
        outst = [sb(f"outst{i}", [128, T], BF16) for i in range(2)]
        vtt = sb("vtt", [128, 8 * 128], BF16)
        pmm = [ps(f"pmm{i}", [128, 512], F32) for i in range(2)]

        for f in range(24):
            s.dma("bq", lambda sy, f=f: sy.dma_start(out=bq[:, f:f + 1], in_=bqkv[f * 128:(f + 1) * 128].unsqueeze(-1)),
                  writes=[("bq", f)])

        for t in range(NT):
            s.dma("x", lambda sy, t=t: sy.dma_start(out=xt[:], in_=x[t * 128:(t + 1) * 128, :]),
                  writes=[("xt",)])
            ln_prepare(s, xt, stats, mv, rstd, negmb, [("xt",)], ("ln", t))
            ln_apply(s, xn_bf2[t % 2][:], xt[:], rstd, negmb, ("ln", t), [("xt",)], ("xn_bf", t % 2))
            for cc in range(NC):
                tpu.t(xnT[:, cc * T + t * 128: cc * T + (t + 1) * 128],
                      xn_bf2[t % 2][:, cc * 128:(cc + 1) * 128],
                      reads=[("xn_bf", t % 2)], writes=[("xnT", t, cc)],
                      evac="scalar" if cc % 2 else "vector")

        xnT_ready = [("xnT", t, cc) for t in range(NT) for cc in range(NC)]

        for f in range(24):
            p = f % 2
            s.dma(f"w{p}", lambda sy, f=f, p=p: sy.dma_start(out=wstage[p][:], in_=wq_r[:, :, f * 128:(f + 1) * 128]),
                  writes=[("wstage", p)])
            s.op("scalar", lambda sc, p=p: sc.copy(out=wbf[p][:], in_=wstage[p][:]),
                 reads=[("wstage", p)], writes=[("wbf", p)])
            for cc in range(NC):
                for th in range(2):
                    s.op("tensor", lambda te, p=p, th=th, cc=cc: te.matmul(
                            pmm[th][:], wbf[p][:, cc * 128:(cc + 1) * 128],
                            xnT[:, cc * T + th * 512: cc * T + (th + 1) * 512],
                            start=(cc == 0), stop=(cc == NC - 1)),
                         reads=[("wbf", p)] + ([] if (cc or th or f) else xnT_ready),
                         writes=[("pmm", th)], tag=f"mm{f}")
            for th in range(2):
                s.op("scalar", lambda sc, f=f, th=th, p=p: sc.activation(
                        out=outst[p][:, th * 512:(th + 1) * 512], in_=pmm[th][:],
                        func=AF.Identity, bias=bq[:, f:f + 1]),
                     reads=[("pmm", th), ("bq", f)], writes=[("outst", p, th)])
            if f < 8:
                s.dma("st", lambda sy, f=f, p=p: sy.dma_start(out=qT_r[:, f, :], in_=outst[p][:]),
                      reads=[("outst", p, 0), ("outst", p, 1)], writes=[], queue="scalar")
            elif f < 16:
                s.dma("st", lambda sy, f=f, p=p: sy.dma_start(out=kT_r[:, f - 8, :], in_=outst[p][:]),
                      reads=[("outst", p, 0), ("outst", p, 1)], writes=[], queue="scalar")
            else:
                dd = f - 16
                for tt in range(NT):
                    tpu.t(vtt[:, tt * 128:(tt + 1) * 128], outst[p][:, tt * 128:(tt + 1) * 128],
                          reads=[("outst", p, 0), ("outst", p, 1)], writes=[("vtt", tt)],
                          evac="scalar" if tt % 2 else "vector")
                    s.dma("vt", lambda sy, tt=tt, dd=dd: sy.dma_start(
                            out=vtok[tt * 128:(tt + 1) * 128, dd * 128:(dd + 1) * 128],
                            in_=vtt[:, tt * 128:(tt + 1) * 128]),
                          reads=[("vtt", tt)], writes=[], queue="scalar")
        if standalone:
            s.emit()
    return nc



def build_l0attn(debug=False, nw=16, s=None, io=None):
    standalone = io is None
    if standalone:
        nc = bass.Bass("TRN2", target_bir_lowering=False, debug=False)
        x = nc.dram_tensor("x", [T, D], F32, kind="ExternalInput").ap()
        qTd = nc.dram_tensor("qT", [1024, T], BF16, kind="ExternalInput").ap()
        kTf = nc.dram_tensor("kTf", [1024, 8192], BF16, kind="ExternalInput").ap()
        vf = nc.dram_tensor("vf", [8192, 1024], BF16, kind="ExternalInput").ap()
        wo = nc.dram_tensor("wo", [1024, D], F32, kind="ExternalInput").ap()
        bo = nc.dram_tensor("bo", [D], F32, kind="ExternalInput").ap()
        y = nc.dram_tensor("y", [T, D], F32, kind="ExternalOutput").ap()
    else:
        nc = s.nc
        x, qTd, kTf, vf, wo, bo, y = (io[k] for k in ("x", "qT", "kTf", "vf", "wo", "bo", "y"))
    if debug:
        dbgO = nc.dram_tensor("dbgO", [128, 8 * 1024], F32, kind="ExternalOutput").ap()
        dbgS = nc.dram_tensor("dbgS", [128, 32], F32, kind="ExternalOutput").ap()
    if debug == 3:
        dbgP = nc.dram_tensor("dbgP", [128, 512], BF16, kind="ExternalOutput").ap()
        dbgPT = nc.dram_tensor("dbgPT", [128, 512], BF16, kind="ExternalOutput").ap()
        dbgO7 = nc.dram_tensor("dbgO7", [128, 1024], F32, kind="ExternalOutput").ap()
    if debug == 2:
        dbgA = nc.dram_tensor("dbgA", [128, 4096], F32, kind="ExternalOutput").ap()
        dbgL = nc.dram_tensor("dbgL", [128, 4096], F32, kind="ExternalOutput").ap()
        dbgO7 = nc.dram_tensor("dbgO7", [128, 1024], F32, kind="ExternalOutput").ap()
        dbgST = nc.dram_tensor("dbgST", [128, 8], F32, kind="ExternalOutput").ap()

    kT_r = kTf.rearrange("(dd p) k -> p dd k", p=128)
    v_r = vf.rearrange("(kt p) d -> p kt d", p=128)
    qT_r = qTd.rearrange("(dd p) t -> p dd t", p=128)
    wo_r = wo.rearrange("(dd p) c -> p dd c", p=128)

    SCALE = 1.0 / 32.0
    NW = nw

    if standalone:
        s = Sched(nc)
    with ExitStack() as ctx:
        sb = lambda *a: ctx.enter_context(nc.sbuf_tensor(*a))
        ps = lambda *a: ctx.enter_context(nc.psum_tensor(*a))
        qT_sb = sb("qT_sb", [128, 8, 1024], BF16)
        O_sb = sb("O_sb", [128, 8, 1024], F32)
        m_sb = sb("m_sb", [128, 8], F32)
        mn_sb = sb("mn_sb", [128, 8], F32)
        raw_sb = sb("raw_sb", [128, 8], F32)
        dl_sb = sb("dl_sb", [128, 8], F32)
        ng_sb = sb("ng_sb", [128, 8], F32)
        corr_sb = sb("corr_sb", [128, 8], F32)
        sum_sb = sb("sum_sb", [128, 8], F32)
        rs_sb = sb("rs_sb", [128, 8], F32)
        rsi_sb = sb("rsi_sb", [128, 8], F32)
        wo_bf = sb("wo_bf", [128, 8, 4096], BF16)
        bo_bcast = sb("bo_bcast", [128, 4096], F32)
        bo_row = sb("bo_row", [1, 4096], F32)
        ones1 = sb("ones1", [1, 128], F32)
        psS = [ps(f"psS{i}", [128, 512], F32) for i in range(2)]
        psO = [[ps(f"psO{i}_{j}", [128, 512], F32) for j in range(2)] for i in range(2)]
        tpu = TPU(s, nc, ctx)

        s.op("vector", lambda v: v.memset(m_sb[:], -30000.0), writes=[("m", qq) for qq in range(8)])
        s.op("vector", lambda v: v.memset(sum_sb[:], 0.0), writes=[("sum", qq) for qq in range(8)])
        s.op("vector", lambda v: v.memset(O_sb[:], 0.0), writes=[("O", qq) for qq in range(8)])
        s.op("vector", lambda v: v.memset(ones1[:], 1.0), writes=[("ones1",)])
        s.dma("cst", lambda sy: sy.dma_start(out=bo_row[:], in_=bo.unsqueeze(0)), writes=[("bo_row",)])
        s.dma("cst", lambda sy: sy.dma_start(out=qT_sb[:], in_=qT_r[:]), writes=[("qT",)])
        for ch in range(8):
            s.op("tensor", lambda te, ch=ch: te.matmul(psO[0][0][:], ones1[:], bo_row[:, ch * 512:(ch + 1) * 512],
                                                       start=True, stop=True),
                 reads=[("ones1",), ("bo_row",)], writes=[("psO", 0, 0)])
            s.op("vector", lambda v, ch=ch: v.tensor_copy(out=bo_bcast[:, ch * 512:(ch + 1) * 512], in_=psO[0][0][:]),
                 reads=[("psO", 0, 0)], writes=[("bo_bc", ch)])
        with ExitStack() as fctx:
            fsb = lambda *a: fctx.enter_context(nc.sbuf_tensor(*a))
            wstage = fsb("wstage", [128, 4096], F32)
            kwin = [fsb(f"kwin{i}", [128, 8, 512], BF16) for i in range(2)]
            vwin = [fsb(f"vwin{i}", [128, 4, 1024], BF16) for i in range(2)]
            p_sb = [fsb(f"p_sb{i}", [128, 512], BF16) for i in range(3)]
            pT_sb = [fsb(f"pT_sb{i}", [128, 512], BF16) for i in range(3)]

            def load_win(kw):
                b = kw % 2
                s.dma(f"kv{b}", lambda sy, kw=kw, b=b: sy.dma_start(
                        out=kwin[b][:], in_=kT_r[:, :, kw * 512:(kw + 1) * 512]),
                      writes=[("kwin", b)], queue="scalar")
                s.dma(f"kv{b}", lambda sy, kw=kw, b=b: sy.dma_start(
                        out=vwin[b][:], in_=v_r[:, kw * 4:(kw + 1) * 4, :]),
                      writes=[("vwin", b)], queue="scalar")

            for dd in range(8):
                s.dma("wo", lambda sy, dd=dd: sy.dma_start(out=wstage[:], in_=wo_r[:, dd, :]),
                      writes=[("wstage",)])
                s.op("scalar", lambda sc, dd=dd: sc.copy(out=wo_bf[:, dd, :], in_=wstage[:]),
                     reads=[("wstage",)], writes=[("wo_bf", dd)])
            load_win(0)
            pend = []
            pend_tp = []

            def emit_av(idx, qq, wb):
                ib = idx % 3
                po = idx % 2
                for kt in range(4):
                    for dh in range(2):
                        s.op("tensor", lambda te, po=po, ib=ib, wb=wb, dh=dh, kt=kt: te.matmul(
                                psO[po][dh][:], pT_sb[ib][:, kt * 128:(kt + 1) * 128],
                                vwin[wb][:, kt, dh * 512:(dh + 1) * 512],
                                start=(kt == 0), stop=(kt == 3)),
                             reads=[("pT", ib), ("vwin", wb)], writes=[("psO", po, dh)])
                for dh in range(2):
                    s.op("vector", lambda v, po=po, qq=qq, dh=dh: v.scalar_tensor_tensor(
                            out=O_sb[:, qq, dh * 512:(dh + 1) * 512], in0=O_sb[:, qq, dh * 512:(dh + 1) * 512],
                            scalar=corr_sb[:, qq:qq + 1], in1=psO[po][dh][:], op0=ALU.mult, op1=ALU.add),
                         reads=[("psO", po, dh), ("corr", qq), ("O", qq)], writes=[("O", qq)])

            def emit_tp(idx, ib):
                for kt in range(4):
                    tpu.t(pT_sb[ib][:, kt * 128:(kt + 1) * 128],
                          p_sb[ib][:, kt * 128:(kt + 1) * 128],
                          reads=[("p", ib)], writes=[("pT", ib)],
                          evac="vector" if kt % 2 else "scalar")

            for kw in range(NW):
                wb = kw % 2
                while pend_tp:
                    emit_tp(*pend_tp.pop(0))
                while pend:
                    emit_av(*pend.pop(0))
                if kw + 1 < NW:
                    load_win(kw + 1)
                for qq in range(8):
                    idx = kw * 8 + qq
                    ib = idx % 3
                    sb2 = idx % 2
                    for dd in range(8):
                        s.op("tensor", lambda te, sb2=sb2, wb=wb, dd=dd, qq=qq: te.matmul(
                                psS[sb2][:], qT_sb[:, dd, qq * 128:(qq + 1) * 128], kwin[wb][:, dd, :],
                                start=(dd == 0), stop=(dd == 7)),
                             reads=[("qT",), ("kwin", wb)], writes=[("psS", sb2)])
                    s.op("vector", lambda v, sb2=sb2, qq=qq: v.reduce_max(
                            out=raw_sb[:, qq:qq + 1], in_=psS[sb2][:], axis=AX.X),
                         reads=[("psS", sb2)], writes=[("raw", qq)])
                    s.op("vector", lambda v, qq=qq: v.tensor_scalar(
                            out=raw_sb[:, qq:qq + 1], in0=raw_sb[:, qq:qq + 1], scalar1=SCALE, scalar2=None, op0=ALU.mult),
                         reads=[("raw", qq)], writes=[("raw", qq)])
                    s.op("vector", lambda v, qq=qq: v.tensor_tensor(
                            out=mn_sb[:, qq:qq + 1], in0=m_sb[:, qq:qq + 1], in1=raw_sb[:, qq:qq + 1], op=ALU.max),
                         reads=[("raw", qq), ("m", qq)], writes=[("mn", qq)])
                    s.op("vector", lambda v, qq=qq: v.tensor_tensor(
                            out=dl_sb[:, qq:qq + 1], in0=m_sb[:, qq:qq + 1], in1=mn_sb[:, qq:qq + 1], op=ALU.subtract),
                         reads=[("m", qq), ("mn", qq)], writes=[("dl", qq)])
                    s.op("vector", lambda v, qq=qq: v.tensor_scalar(
                            out=ng_sb[:, qq:qq + 1], in0=mn_sb[:, qq:qq + 1], scalar1=-1.0, scalar2=None, op0=ALU.mult),
                         reads=[("mn", qq)], writes=[("ng", qq)])
                    s.op("vector", lambda v, qq=qq: v.tensor_copy(
                            out=m_sb[:, qq:qq + 1], in_=mn_sb[:, qq:qq + 1]),
                         reads=[("mn", qq), ("dl", qq)], writes=[("m", qq)])
                    s.op("scalar", lambda sc, qq=qq: sc.activation(
                            out=corr_sb[:, qq:qq + 1], in_=dl_sb[:, qq:qq + 1], func=AF.Exp),
                         reads=[("dl", qq)], writes=[("corr", qq)])
                    s.op("scalar", lambda sc, sb2=sb2, ib=ib, qq=qq: sc.activation(
                            out=p_sb[ib][:], in_=psS[sb2][:], func=AF.Exp, bias=ng_sb[:, qq:qq + 1],
                            scale=SCALE, accum_out=rs_sb[:, qq:qq + 1]),
                         reads=[("psS", sb2), ("ng", qq)], writes=[("p", ib), ("rs", qq)])
                    s.op("vector", lambda v, qq=qq: v.scalar_tensor_tensor(
                            out=sum_sb[:, qq:qq + 1], in0=sum_sb[:, qq:qq + 1], scalar=corr_sb[:, qq:qq + 1],
                            in1=rs_sb[:, qq:qq + 1], op0=ALU.mult, op1=ALU.add),
                         reads=[("rs", qq), ("corr", qq), ("sum", qq)], writes=[("sum", qq)])
                    pend_tp.append((idx, ib))
                    if len(pend_tp) > 1:
                        emit_tp(*pend_tp.pop(0))
                    pend.append((idx, qq, wb))
                    if len(pend) > 2:
                        emit_av(*pend.pop(0))
            while pend_tp:
                emit_tp(*pend_tp.pop(0))
            while pend:
                emit_av(*pend.pop(0))
            if debug == 3:
                s.dma("dbg", lambda sy: sy.dma_start(out=dbgP[:], in_=p_sb[1][:]), reads=[("p", 1)], writes=[])
                s.dma("dbg", lambda sy: sy.dma_start(out=dbgPT[:], in_=pT_sb[1][:]), reads=[("pT", 1)], writes=[])
                s.dma("dbg", lambda sy: sy.dma_start(out=dbgO7[:], in_=O_sb[:, 7, :]), reads=[("O", 7)], writes=[])

        s.barrier()
        if debug == 1:
            s.dma("dbg", lambda sy: sy.dma_start(out=dbgO[:], in_=O_sb[:].rearrange("p a b -> p (a b)")),
                  reads=[("O", qq) for qq in range(8)], writes=[])
            s.dma("dbg", lambda sy: sy.dma_start(out=dbgS[:, 0:8], in_=sum_sb[:]),
                  reads=[("sum", qq) for qq in range(8)], writes=[])
            s.dma("dbg", lambda sy: sy.dma_start(out=dbgS[:, 8:16], in_=m_sb[:]),
                  reads=[("m", qq) for qq in range(8)], writes=[])
            s.dma("dbg", lambda sy: sy.dma_start(out=dbgS[:, 16:24], in_=corr_sb[:]),
                  reads=[("corr", qq) for qq in range(8)], writes=[])
            s.dma("dbg", lambda sy: sy.dma_start(out=dbgS[:, 24:32], in_=rs_sb[:]),
                  reads=[("rs", qq) for qq in range(8)], writes=[])
        s.op("vector", lambda v: v.reciprocal(out=rsi_sb[:], in_=sum_sb[:]),
             reads=[("sum", qq) for qq in range(8)], writes=[("rsi",)])
        with ExitStack() as ectx:
            esb = lambda *a: ectx.enter_context(nc.sbuf_tensor(*a))
            o_bf = esb("o_bf", [128, 1024], BF16)
            oT_sb = esb("oT_sb", [128, 8, 128], BF16)
            A_sb = esb("A_sb", [128, 4096], F32)
            lnt = esb("lnt", [128, 4096], F32)
            xep = esb("xep", [128, 4096], F32)
            stats = esb("statsB", [128, 48], F32)
            mv = esb("mvB", [128, 2], F32)
            rstd = esb("rstdB", [128, 1], F32)
            negmb = esb("negmbB", [128, 1], F32)
            for qq in range(8):
                s.dma("x", lambda sy, qq=qq: sy.dma_start(out=xep[:], in_=x[qq * 128:(qq + 1) * 128, :]),
                      writes=[("xep",)])
                s.op("scalar", lambda sc, qq=qq: sc.activation(
                        out=o_bf[:], in_=O_sb[:, qq, :], func=AF.Identity, scale=rsi_sb[:, qq:qq + 1]),
                     reads=[("O", qq), ("rsi",)], writes=[("o_bf",)])
                for dd in range(8):
                    tpu.t(oT_sb[:, dd, :], o_bf[:, dd * 128:(dd + 1) * 128],
                          reads=[("o_bf",)], writes=[("oT", dd)],
                          evac="vector" if dd % 2 else "scalar")
                for ch in range(8):
                    cb = ch % 2
                    for dd in range(8):
                        s.op("tensor", lambda te, cb=cb, ch=ch, dd=dd: te.matmul(
                                psO[0][cb][:], oT_sb[:, dd, :], wo_bf[:, dd, ch * 512:(ch + 1) * 512],
                                start=(dd == 0), stop=(dd == 7)),
                             reads=[("oT", dd), ("wo_bf", dd)], writes=[("psO", 0, cb)])
                    s.op("vector", lambda v, cb=cb, ch=ch: v.tensor_tensor(
                            out=A_sb[:, ch * 512:(ch + 1) * 512], in0=psO[0][cb][:],
                            in1=bo_bcast[:, ch * 512:(ch + 1) * 512], op=ALU.add),
                         reads=[("psO", 0, cb), ("bo_bc", ch)], writes=[("A", ch)])
                ln_prepare(s, A_sb, stats, mv, rstd, negmb, [("A", ch) for ch in range(8)], ("lnB", qq))
                ln_apply(s, lnt[:], A_sb[:], rstd, negmb, ("lnB", qq),
                         [("A", ch) for ch in range(8)], ("lnt",))
                s.op("vector", lambda v, qq=qq: v.tensor_tensor(out=xep[:], in0=xep[:], in1=lnt[:], op=ALU.add),
                     reads=[("xep",), ("lnt",)], writes=[("xep",)])
                s.dma("st", lambda sy, qq=qq: sy.dma_start(out=y[qq * 128:(qq + 1) * 128, :], in_=xep[:]),
                      reads=[("xep",)], writes=[])
            if debug == 2:
                s.dma("dbg", lambda sy: sy.dma_start(out=dbgA[:], in_=A_sb[:]),
                      reads=[("A", ch) for ch in range(8)], writes=[])
                s.dma("dbg", lambda sy: sy.dma_start(out=dbgL[:], in_=lnt[:]), reads=[("lnt",)], writes=[])
                s.dma("dbg", lambda sy: sy.dma_start(out=dbgO7[:], in_=O_sb[:, 7, :]),
                      reads=[("O", 7)], writes=[])
                s.dma("dbg", lambda sy: sy.dma_start(out=dbgST[:], in_=rsi_sb[:]), reads=[("rsi",)], writes=[])
        if standalone:
            s.emit()
    return nc



def build_l0ffn(debug=False, s=None, io=None):
    standalone = io is None
    if standalone:
        nc = bass.Bass("TRN2", target_bir_lowering=False, debug=False)
        yd = nc.dram_tensor("y", [T, D], F32, kind="ExternalInput").ap()
        rw = nc.dram_tensor("rw", [64, D], F32, kind="ExternalInput").ap()
        rb = nc.dram_tensor("rb", [64], F32, kind="ExternalInput").ap()
        eb = nc.dram_tensor("eb", [64, D], F32, kind="ExternalInput").ap()
        y2 = nc.dram_tensor("y2", [T, D], F32, kind="ExternalOutput").ap()
    else:
        nc = s.nc
        yd, rw, rb, eb, y2 = (io[k] for k in ("y", "rw", "rb", "eb", "y2"))
    if debug:
        dbgL = nc.dram_tensor("dbgL", [128, 64], F32, kind="ExternalOutput").ap()
        dbgSC = nc.dram_tensor("dbgSC", [128, 64], F32, kind="ExternalOutput").ap()
        dbgTM = nc.dram_tensor("dbgTM", [128, 64], F32, kind="ExternalOutput").ap()
        dbgAC = nc.dram_tensor("dbgAC", [128, 2], F32, kind="ExternalOutput").ap()
        dbgO = nc.dram_tensor("dbgO", [128, D], F32, kind="ExternalOutput").ap()

    K_TOP = 8
    if standalone:
        s = Sched(nc)
    with ExitStack() as ctx:
        sb = lambda *a: ctx.enter_context(nc.sbuf_tensor(*a))
        ps = lambda *a: ctx.enter_context(nc.psum_tensor(*a))
        tpu = TPU(s, nc, ctx)
        rw_st = sb("rw_st", [64, D], F32)
        rw_bf = sb("rw_bf", [64, D], BF16)
        rwT = sb("rwT", [128, NC * 64], BF16)
        eb_st = sb("eb_st", [64, D], F32)
        eb_bf = sb("eb_bf", [64, D], BF16)
        rb_row = sb("rb_row", [1, 64], F32)
        rb_bc = sb("rb_bc", [128, 64], F32)
        ones1 = sb("ones1C", [1, 128], F32)
        yt = sb("ytC", [128, D], F32)
        xn32 = sb("xn32", [128, D], F32)
        xn_bf = sb("xn_bfC", [128, D], BF16)
        xnT = sb("xnTC", [128, D], BF16)
        lg = sb("lg", [128, 64], F32)
        ngm = sb("ngmC", [128, 1], F32)
        ex = sb("exC", [128, 64], F32)
        rsx = sb("rsxC", [128, 1], F32)
        sc_sb = sb("sc_sb", [128, 64], F32)
        wrk = sb("wrk", [128, 64], F32)
        msk = sb("mskC", [128, 64], F32)
        tmask = sb("tmaskC", [128, 64], F32)
        mx = sb("mxC", [128, 1], F32)
        acc = sb("accC", [128, 1], F32)
        wd32 = sb("wd32", [128, 64], F32)
        wd_bf = sb("wd_bf", [128, 128], BF16)
        wdT = sb("wdT", [128, 128], BF16)
        o_sb = sb("o_sbC", [128, D], F32)
        lnt = sb("lntC", [128, D], F32)
        stats = sb("statsC", [128, 48], F32)
        mv = sb("mvC", [128, 2], F32)
        rstd = sb("rstdC", [128, 1], F32)
        negmb = sb("negmbC", [128, 1], F32)
        psL = ps("psL", [128, 64], F32)
        psF = [ps(f"psF{i}", [128, 512], F32) for i in range(2)]

        s.op("vector", lambda v: v.memset(ones1[:], 1.0), writes=[("ones1",)])
        s.op("vector", lambda v: v.memset(wd_bf[:], 0.0), writes=[("wd_bf",)])
        s.dma("cst", lambda sy: sy.dma_start(out=rw_st[:], in_=rw[:]), writes=[("rw_st",)])
        s.dma("cst", lambda sy: sy.dma_start(out=eb_st[:], in_=eb[:]), writes=[("eb_st",)])
        s.dma("cst", lambda sy: sy.dma_start(out=rb_row[:], in_=rb.unsqueeze(0)), writes=[("rb_row",)])
        s.op("scalar", lambda sc: sc.copy(out=rw_bf[:], in_=rw_st[:]), reads=[("rw_st",)], writes=[("rw_bf",)])
        s.op("scalar", lambda sc: sc.copy(out=eb_bf[:], in_=eb_st[:]), reads=[("eb_st",)], writes=[("eb_bf",)])
        for cc in range(NC):
            tpu.t(rwT[:, cc * 64:(cc + 1) * 64], rw_bf[:, cc * 128:(cc + 1) * 128],
                  reads=[("rw_bf",)], writes=[("rwT", cc)], p=64,
                  evac="scalar" if cc % 2 else "vector")
        s.op("tensor", lambda te: te.matmul(psL[:], ones1[:], rb_row[:], start=True, stop=True),
             reads=[("ones1",), ("rb_row",)], writes=[("psL",)])
        s.op("vector", lambda v: v.tensor_copy(out=rb_bc[:], in_=psL[:]),
             reads=[("psL",)], writes=[("rb_bc",)])

        for t in range(NT):
            k = ("C", t)
            s.dma("x", lambda sy, t=t: sy.dma_start(out=yt[:], in_=yd[t * 128:(t + 1) * 128, :]),
                  writes=[("yt",)])
            ln_prepare(s, yt, stats, mv, rstd, negmb, [("yt",)], ("lnC", t))
            ln_apply(s, xn32[:], yt[:], rstd, negmb, ("lnC", t), [("yt",)], ("xn32",))
            s.op("scalar", lambda sc: sc.copy(out=xn_bf[:], in_=xn32[:]), reads=[("xn32",)], writes=[("xn_bf",)])
            for cc in range(NC):
                tpu.t(xnT[:, cc * 128:(cc + 1) * 128], xn_bf[:, cc * 128:(cc + 1) * 128],
                      reads=[("xn_bf",)], writes=[("xnT", cc)],
                      evac="scalar" if cc % 2 else "vector")
            for cc in range(NC):
                s.op("tensor", lambda te, cc=cc: te.matmul(
                        psL[:], xnT[:, cc * 128:(cc + 1) * 128], rwT[:, cc * 64:(cc + 1) * 64],
                        start=(cc == 0), stop=(cc == NC - 1)),
                     reads=[("xnT", cc), ("rwT", cc)], writes=[("psL",)])
            s.op("vector", lambda v: v.tensor_tensor(out=lg[:], in0=psL[:], in1=rb_bc[:], op=ALU.add),
                 reads=[("psL",), ("rb_bc",)], writes=[("lg",)])
            s.op("vector", lambda v: v.reduce_max(out=ngm[:], in_=lg[:], axis=AX.X, negate=True),
                 reads=[("lg",)], writes=[("ngm",)])
            s.op("scalar", lambda sc: sc.activation(out=ex[:], in_=lg[:], func=AF.Exp,
                                                    bias=ngm[:, 0:1], accum_out=rsx[:, 0:1]),
                 reads=[("lg",), ("ngm",)], writes=[("ex",), ("rsx",)])
            s.op("vector", lambda v: v.reciprocal(out=rsx[:], in_=rsx[:]), reads=[("rsx",)], writes=[("rsxi",)])
            s.op("scalar", lambda sc: sc.activation(out=sc_sb[:], in_=ex[:], func=AF.Identity, scale=rsx[:, 0:1]),
                 reads=[("ex",), ("rsxi",)], writes=[("sc",)])
            s.op("vector", lambda v: v.tensor_copy(out=wrk[:], in_=lg[:]), reads=[("lg",)], writes=[("wrk",)])
            s.op("vector", lambda v: v.memset(tmask[:], 0.0), writes=[("tmask",)])
            for it in range(K_TOP):
                s.op("vector", lambda v: v.reduce_max(out=mx[:], in_=wrk[:], axis=AX.X),
                     reads=[("wrk",)], writes=[("mx",)])
                s.op("vector", lambda v: v.tensor_scalar(out=msk[:], in0=wrk[:], scalar1=mx[:, 0:1],
                                                         scalar2=None, op0=ALU.is_ge),
                     reads=[("wrk",), ("mx",)], writes=[("msk",)])
                s.op("vector", lambda v: v.scalar_tensor_tensor(out=wrk[:], in0=msk[:], scalar=-1000.0, in1=wrk[:],
                                                                op0=ALU.mult, op1=ALU.add),
                     reads=[("msk",), ("wrk",)], writes=[("wrk",)])
                s.op("vector", lambda v: v.tensor_tensor(out=tmask[:], in0=tmask[:], in1=msk[:], op=ALU.add),
                     reads=[("tmask",), ("msk",)], writes=[("tmask",)])
            s.op("vector", lambda v: v.tensor_tensor(out=wd32[:], in0=sc_sb[:], in1=tmask[:], op=ALU.mult),
                 reads=[("sc",), ("tmask",)], writes=[("wd32",)])
            s.op("vector", lambda v: v.reduce_sum(out=acc[:], in_=wd32[:], axis=AX.X),
                 reads=[("wd32",)], writes=[("acc",)])
            s.op("vector", lambda v: v.tensor_copy(out=wd_bf[:, 0:64], in_=wd32[:]),
                 reads=[("wd32",), ("wd_bf",)], writes=[("wd_bf",)])
            tpu.t(wdT[:], wd_bf[:], reads=[("wd_bf",)], writes=[("wdT",)], evac="vector")
            for ch in range(8):
                cb = ch % 2
                s.op("tensor", lambda te, cb=cb, ch=ch: te.matmul(
                        psF[cb][:], wdT[0:64, :], eb_bf[:, ch * 512:(ch + 1) * 512], start=True, stop=True),
                     reads=[("wdT",), ("eb_bf",)], writes=[("psF", cb)])
                s.op("vector", lambda v, cb=cb, ch=ch: v.scalar_tensor_tensor(
                        out=o_sb[:, ch * 512:(ch + 1) * 512], in0=xn32[:, ch * 512:(ch + 1) * 512],
                        scalar=acc[:, 0:1], in1=psF[cb][:], op0=ALU.mult, op1=ALU.add),
                     reads=[("xn32",), ("acc",), ("psF", cb)], writes=[("o", ch)])
            ln_prepare(s, o_sb, stats, mv, rstd, negmb, [("o", ch) for ch in range(8)], ("lnC2", t))
            ln_apply(s, lnt[:], o_sb[:], rstd, negmb, ("lnC2", t), [("o", ch) for ch in range(8)], ("lnt",))
            s.op("vector", lambda v: v.tensor_tensor(out=yt[:], in0=yt[:], in1=lnt[:], op=ALU.add),
                 reads=[("yt",), ("lnt",)], writes=[("yt",)])
            s.dma("st", lambda sy, t=t: sy.dma_start(out=y2[t * 128:(t + 1) * 128, :], in_=yt[:]),
                  reads=[("yt",)], writes=[])
        if debug:
            s.dma("dbg", lambda sy: sy.dma_start(out=dbgL[:], in_=lg[:]), reads=[("lg",)], writes=[])
            s.dma("dbg", lambda sy: sy.dma_start(out=dbgSC[:], in_=sc_sb[:]), reads=[("sc",)], writes=[])
            s.dma("dbg", lambda sy: sy.dma_start(out=dbgTM[:], in_=tmask[:]), reads=[("tmask",)], writes=[])
            s.dma("dbg", lambda sy: sy.dma_start(out=dbgAC[:, 0:1], in_=acc[:], allow_slow_non_contiguous=True), reads=[("acc",)], writes=[])
            s.dma("dbg", lambda sy: sy.dma_start(out=dbgO[:], in_=o_sb[:]), reads=[("o", ch) for ch in range(8)], writes=[])
        if standalone:
            s.emit()
    return nc



def build_l1qkv(parts=(1, 2, 3, 4), s=None, io=None):
    standalone = io is None
    if standalone:
        nc = bass.Bass("TRN2", target_bir_lowering=False, debug=False)
        yd = nc.dram_tensor("y", [T, D], F32, kind="ExternalInput").ap()
        wq = nc.dram_tensor("wq", [D, 4096], F32, kind="ExternalInput").ap()
        wk = nc.dram_tensor("wk", [D, 1024], F32, kind="ExternalInput").ap()
        wv = nc.dram_tensor("wv", [D, 1024], F32, kind="ExternalInput").ap()
        pos = nc.dram_tensor("pos", [1, T], F32, kind="ExternalInput").ap()
        qTr = nc.dram_tensor("qTr", [4096, T], BF16, kind="ExternalOutput").ap()
        kTr = nc.dram_tensor("kTr", [1024, T], BF16, kind="ExternalOutput").ap()
        vtok = nc.dram_tensor("vtok", [T, 1024], BF16, kind="ExternalOutput").ap()
    else:
        nc = s.nc
        yd, wq, wk, wv, pos, qTr, kTr, vtok = (io[k] for k in ("y", "wq", "wk", "wv", "pos", "qTr", "kTr", "vtok"))

    wq_r = wq.rearrange("(cc p) f -> p cc f", p=128)
    wk_r = wk.rearrange("(cc p) f -> p cc f", p=128)
    wv_r = wv.rearrange("(cc p) f -> p cc f", p=128)
    qTr_r = qTr.rearrange("(f p) t -> p f t", p=128)
    kTr_r = kTr.rearrange("(f p) t -> p f t", p=128)

    if standalone:
        s = Sched(nc)
    with ExitStack() as ctx:
        sb = lambda *a: ctx.enter_context(nc.sbuf_tensor(*a))
        ps = lambda *a: ctx.enter_context(nc.psum_tensor(*a))
        tpu = TPU(s, nc, ctx)
        xt = sb("xtD", [128, D], F32)
        stats = sb("statsD", [128, 48], F32)
        mv = sb("mvD", [128, 2], F32)
        rstd = sb("rstdD", [128, 1], F32)
        negmb = sb("negmbD", [128, 1], F32)
        xn_bf = sb("xn_bfD", [128, D], BF16)
        xnT = sb("xnTD", [128, NC * T], BF16)
        wstage = [sb(f"wstD{i}", [128, NC * 128], F32) for i in range(2)]
        wbf = [sb(f"wbfD{i}", [128, NC * 128], BF16) for i in range(2)]
        rope_bf = [sb(f"ropeD{i}", [128, T], BF16) for i in range(4)]
        t1 = sb("t1D", [64, T], F32)
        t2 = sb("t2D", [64, T], F32)
        jrow = sb("jrowD", [1, 64], F32)
        pos_sb = sb("posD", [1, T], F32)
        ang = sb("angD", [64, T], F32)
        ang2 = sb("ang2D", [64, T], F32)
        kt_ = sb("ktD", [64, T], F32)
        rt_ = sb("rtD", [64, T], F32)
        sin_sb = sb("sinD", [64, T], F32)
        cos_sb = sb("cosD", [64, T], F32)
        vT = sb("vTD", [128, 8 * T], BF16)
        vtt = sb("vttD", [128, 8 * 128], BF16)
        psq = [ps(f"psqD{i}", [128, 1024], F32) for i in range(3)]

        s.op("gpsimd", lambda g: g.iota(jrow[:], [[1, 64]], channel_multiplier=0,
                                        allow_small_or_imprecise_dtypes=True), writes=[("jrow",)])
        s.op("scalar", lambda sc: sc.activation(out=jrow[:], in_=jrow[:], func=AF.Exp,
                                                scale=-float(np.log(10000.0) / 64.0)),
             reads=[("jrow",)], writes=[("invf",)])
        s.dma("cst", lambda sy: sy.dma_start(out=pos_sb[:], in_=pos[:]), writes=[("pos",)])
        for h in range(2):
            s.op("tensor", lambda te, h=h: te.matmul(psq[0][0:64, h * 512:(h + 1) * 512], jrow[:],
                                                     pos_sb[:, h * 512:(h + 1) * 512], start=True, stop=True),
                 reads=[("invf",), ("pos",)], writes=[("psq", 0)])
        s.op("vector", lambda v: v.tensor_copy(out=ang[:], in_=psq[0][0:64, :]), reads=[("psq", 0)], writes=[("ang",)])
        s.op("vector", lambda v: v.tensor_scalar(out=ang2[:], in0=ang[:], scalar1=float(np.pi / 2),
                                                 scalar2=None, op0=ALU.add),
             reads=[("ang",)], writes=[("ang2",)])

        def wrap_sin(src_ap, srckey, out_ap, outkey):
            s.op("vector", lambda v: v.tensor_scalar(out=kt_[:], in0=src_ap, scalar1=float(1.0 / TWO_PI),
                                                     scalar2=MAGIC, op0=ALU.mult, op1=ALU.add),
                 reads=[srckey], writes=[("kt1",)])
            s.op("vector", lambda v: v.tensor_scalar(out=kt_[:], in0=kt_[:], scalar1=-MAGIC,
                                                     scalar2=None, op0=ALU.add),
                 reads=[("kt1",)], writes=[("kt2",)])
            s.op("vector", lambda v: v.scalar_tensor_tensor(out=rt_[:], in0=kt_[:], scalar=-CW_C1,
                                                            in1=src_ap, op0=ALU.mult, op1=ALU.add),
                 reads=[("kt2",), srckey], writes=[("rt1",)])
            s.op("vector", lambda v: v.scalar_tensor_tensor(out=rt_[:], in0=kt_[:], scalar=-CW_C2,
                                                            in1=rt_[:], op0=ALU.mult, op1=ALU.add),
                 reads=[("kt2",), ("rt1",)], writes=[("rt2",)])
            s.op("scalar", lambda sc: sc.activation(out=out_ap, in_=rt_[:], func=AF.Sin),
                 reads=[("rt2",)], writes=[outkey])

        if 1 in parts:
            wrap_sin(ang[:], ("ang",), sin_sb[:], ("sin",))
            wrap_sin(ang2[:], ("ang2",), cos_sb[:], ("cos",))
        else:
            s.op("vector", lambda v: v.memset(sin_sb[:], 0.1), writes=[("sin",)])
            s.op("vector", lambda v: v.memset(cos_sb[:], 0.9), writes=[("cos",)])

        for t in range(NT):
            s.dma("x", lambda sy, t=t: sy.dma_start(out=xt[:], in_=yd[t * 128:(t + 1) * 128, :]),
                  writes=[("xt",)])
            ln_prepare(s, xt, stats, mv, rstd, negmb, [("xt",)], ("lnD", t))
            ln_apply(s, xn_bf[:], xt[:], rstd, negmb, ("lnD", t), [("xt",)], ("xn_bf",))
            for cc in range(NC):
                tpu.t(xnT[:, cc * T + t * 128: cc * T + (t + 1) * 128],
                      xn_bf[:, cc * 128:(cc + 1) * 128],
                      reads=[("xn_bf",)], writes=[("xnT", t, cc)],
                      evac="scalar" if cc % 2 else "vector")
        xnT_ready = [("xnT", t, cc) for t in range(NT) for cc in range(NC)]

        chunks = [("q", f) for f in range(32)] + [("k", f) for f in range(8)] + [("v", f) for f in range(8)]
        first = True
        for ci, (kind, f) in enumerate(chunks):
            p = ci % 2
            pq = ci % 3
            pr = ci % 4
            src = {"q": wq_r, "k": wk_r, "v": wv_r}[kind]
            s.dma(f"w{p}", lambda sy, src=src, f=f, p=p: sy.dma_start(
                    out=wstage[p][:], in_=src[:, :, f * 128:(f + 1) * 128]),
                  writes=[("wst", p)])
            s.op("scalar", lambda sc, p=p: sc.copy(out=wbf[p][:], in_=wstage[p][:]),
                 reads=[("wst", p)], writes=[("wbf", p)])
            for cc in range(NC):
                for th in range(2):
                    s.op("tensor", lambda te, pq=pq, p=p, th=th, cc=cc: te.matmul(
                            psq[pq][:, th * 512:(th + 1) * 512], wbf[p][:, cc * 128:(cc + 1) * 128],
                            xnT[:, cc * T + th * 512: cc * T + (th + 1) * 512],
                            start=(cc == 0), stop=(cc == NC - 1)),
                         reads=[("wbf", p)] + (xnT_ready if first else []),
                         writes=[("psq", pq)])
                    first = False
            if kind in ("q", "k") and 3 not in parts:
                dst = qTr_r if kind == "q" else kTr_r
                s.op("scalar", lambda sc, p=p: sc.copy(out=rope_bf[p][:], in_=psq[p][:]),
                     reads=[("psq", p)], writes=[("rope", p, 0), ("rope", p, 1)])
                s.dma("st", lambda sy, dst=dst, f=f, p=p: sy.dma_start(out=dst[:, f, :], in_=rope_bf[p][:]),
                      reads=[("rope", p, 0), ("rope", p, 1)], writes=[])
            elif kind in ("q", "k"):
                s.op("vector", lambda v, pq=pq: v.tensor_tensor(out=t1[:], in0=psq[pq][0:64, :], in1=cos_sb[:], op=ALU.mult),
                     reads=[("psq", pq), ("cos",)], writes=[("t1",)])
                s.op("vector", lambda v, pq=pq: v.tensor_tensor(out=t2[:], in0=psq[pq][64:128, :], in1=sin_sb[:], op=ALU.mult),
                     reads=[("psq", pq), ("sin",)], writes=[("t2",)])
                s.op("vector", lambda v, pr=pr: v.tensor_tensor(out=rope_bf[pr][0:64, :], in0=t1[:], in1=t2[:], op=ALU.subtract),
                     reads=[("t1",), ("t2",)], writes=[("rope", pr, 0)])
                s.op("vector", lambda v, pq=pq: v.tensor_tensor(out=t1[:], in0=psq[pq][0:64, :], in1=sin_sb[:], op=ALU.mult),
                     reads=[("psq", pq), ("sin",)], writes=[("t1",)])
                s.op("vector", lambda v, pq=pq: v.tensor_tensor(out=t2[:], in0=psq[pq][64:128, :], in1=cos_sb[:], op=ALU.mult),
                     reads=[("psq", pq), ("cos",)], writes=[("t2",)])
                s.op("vector", lambda v, pr=pr: v.tensor_tensor(out=rope_bf[pr][64:128, :], in0=t1[:], in1=t2[:], op=ALU.add),
                     reads=[("t1",), ("t2",)], writes=[("rope", pr, 1)])
                dst = qTr_r if kind == "q" else kTr_r
                s.dma(f"st{pr % 2}", lambda sy, dst=dst, f=f, pr=pr: sy.dma_start(out=dst[:, f, :], in_=rope_bf[pr][:]),
                      reads=[("rope", pr, 0), ("rope", pr, 1)], writes=[], queue="gpsimd")
            else:
                dd = f
                s.op("scalar", lambda sc, pq=pq, dd=dd: sc.copy(out=vT[:, dd * T:(dd + 1) * T], in_=psq[pq][:]),
                     reads=[("psq", pq)], writes=[("vT", dd)])

        for tt in range(NT):
            for dd in range(8):
                tpu.t(vtt[:, dd * 128:(dd + 1) * 128],
                      vT[:, dd * T + tt * 128: dd * T + (tt + 1) * 128],
                      reads=[("vT", dd)], writes=[("vtt", dd)],
                      evac="scalar" if dd % 2 else "vector")
            s.dma("vt", lambda sy, tt=tt: sy.dma_start(
                    out=vtok[tt * 128:(tt + 1) * 128, :], in_=vtt[:]),
                  reads=[("vtt", dd) for dd in range(8)], writes=[("vttf",)])
        if standalone:
            s.emit()
    return nc



def build_l1attn(s=None, io=None):
    standalone = io is None
    if standalone:
        nc = bass.Bass("TRN2", target_bir_lowering=False, debug=False)
        yd = nc.dram_tensor("y", [T, D], F32, kind="ExternalInput").ap()
        qTr = nc.dram_tensor("qTr", [4096, T], BF16, kind="ExternalInput").ap()
        kw = nc.dram_tensor("kw", [1024, 2048], BF16, kind="ExternalInput").ap()
        vw = nc.dram_tensor("vw", [2048, 1024], BF16, kind="ExternalInput").ap()
        wo = nc.dram_tensor("wo", [4096, D], F32, kind="ExternalInput").ap()
        qpos = nc.dram_tensor("qpos", [1, T], F32, kind="ExternalInput").ap()
        kpos = nc.dram_tensor("kpos", [1, 2048], F32, kind="ExternalInput").ap()
        y3 = nc.dram_tensor("y3", [T, D], F32, kind="ExternalOutput").ap()
    else:
        nc = s.nc
        yd, qTr, kw, vw, wo, qpos, kpos, y3 = (io[k] for k in ("y", "qTr", "kw", "vw", "wo", "qpos", "kpos", "y3"))

    kw_r = kw.rearrange("(h p) k -> p h k", p=128)
    vw_r = vw.rearrange("(kt p) d -> p kt d", p=128)
    qTr_r = qTr.rearrange("(f p) t -> p f t", p=128)
    wo_r = wo.rearrange("(dd p) c -> p dd c", p=128)

    SM = float(128.0 ** -0.5)
    NSPAN = 9

    if standalone:
        s = Sched(nc)
    with ExitStack() as ctx:
        sb = lambda *a: ctx.enter_context(nc.sbuf_tensor(*a))
        ps = lambda *a: ctx.enter_context(nc.psum_tensor(*a))
        oT_all = sb("oT_all", [128, NC * T], BF16)
        psS = [ps("psSE0", [128, 1536], F32)]
        psO = [ps("psOE0", [128, 128], F32)]

        with ExitStack() as actx:
            asb = lambda *a: actx.enter_context(nc.sbuf_tensor(*a))
            tpu = TPU(s, nc, actx, nbanks=3)
            kwin = asb("kwinE", [128, 8, 2048], BF16)
            vwin = asb("vwinE", [128, 16, 1024], BF16)
            qt_sb = [asb(f"qtE{i}", [128, 32 * 128], BF16) for i in range(2)]
            o_bf = asb("o_bfE", [128, 4096], BF16)
            p_sb = [asb(f"p_sbE{i}", [128, 1152], BF16) for i in range(3)]
            Ssb = [asb(f"SsbE{i}", [128, 1152], F32) for i in range(3)]
            pT_sb = [asb(f"pT_sbE{i}", [128, 1152], BF16) for i in range(3)]
            kposb = asb("kposbE", [128, 2048], F32)
            qpos_sb = asb("qposE", [128, 8], F32)
            qpm_sb = asb("qpmE", [128, 8], F32)
            kpr = asb("kprE", [1, 2048], F32)
            ones1 = asb("ones1E", [1, 128], F32)
            masks = asb("masksE", [128, NSPAN * 128], F32)
            m1_sb = asb("m1E", [128, 128], F32)
            ngx = asb("ngxE", [128, 3], F32)
            rs_sb = asb("rsE", [128, 3], F32)
            rsi_sb = asb("rsiE", [128, 3], F32)

            s.op("vector", lambda v: v.memset(ones1[:], 1.0), writes=[("ones1",)])
            s.dma("cst", lambda sy: sy.dma_start(out=kpr[:], in_=kpos[:]), writes=[("kpr",)])
            s.dma("cst", lambda sy: sy.dma_start(out=kwin[:], in_=kw_r[:]), writes=[("kwin",)])
            s.dma("cst", lambda sy: sy.dma_start(out=vwin[:], in_=vw_r[:]), writes=[("vwin",)])
            for qq in range(8):
                s.dma("cst", lambda sy, qq=qq: sy.dma_start(
                        out=qpos_sb[:, qq:qq + 1], in_=qpos[0, qq * 128:(qq + 1) * 128].unsqueeze(-1)),
                      writes=[("qpos", qq)])
            s.op("vector", lambda v: v.tensor_scalar(out=qpm_sb[:], in0=qpos_sb[:], scalar1=-1024.0,
                                                     scalar2=None, op0=ALU.add),
                 reads=[("qpos", qq) for qq in range(8)], writes=[("qpm",)])
            for h4 in range(3):
                s.op("tensor", lambda te, h4=h4: te.matmul(
                        psS[0][:, h4 * 512:(h4 + 1) * 512], ones1[:], kpr[:, h4 * 512:(h4 + 1) * 512],
                        start=True, stop=True),
                     reads=[("ones1",), ("kpr",)], writes=[("psS", 0)])
            s.op("vector", lambda v: v.tensor_copy(out=kposb[:, 0:1536], in_=psS[0][:]),
                 reads=[("psS", 0)], writes=[("kposb",)])
            s.op("tensor", lambda te: te.matmul(psS[0][:, 0:512], ones1[:], kpr[:, 1536:2048],
                                                start=True, stop=True),
                 reads=[("ones1",), ("kpr",)], writes=[("psS", 0)])
            s.op("vector", lambda v: v.tensor_copy(out=kposb[:, 1536:2048], in_=psS[0][:, 0:512]),
                 reads=[("psS", 0)], writes=[("kposb",)])

            pend = []
            pend_tp = []

            def emit_av(idx, qq, h, ib):
                kvh = h // 4
                for kt in range(NSPAN):
                    s.op("tensor", lambda te, ib=ib, qq=qq, kvh=kvh, kt=kt: te.matmul(
                            psO[0][:], pT_sb[ib][:, kt * 128:(kt + 1) * 128],
                            vwin[:, qq + kt, kvh * 128:(kvh + 1) * 128],
                            start=(kt == 0), stop=(kt == NSPAN - 1)),
                         reads=[("pT", ib), ("vwin",)], writes=[("psO", 0)])
                s.op("vector", lambda v, ib=ib: v.reciprocal(out=rsi_sb[:, ib:ib + 1], in_=rs_sb[:, ib:ib + 1]),
                     reads=[("rs", ib)], writes=[("rsi", ib)])
                s.op("scalar", lambda sc, ib=ib, h=h: sc.activation(
                        out=o_bf[:, h * 128:(h + 1) * 128], in_=psO[0][:], func=AF.Identity,
                        scale=rsi_sb[:, ib:ib + 1]),
                     reads=[("psO", 0), ("rsi", ib)], writes=[("o", h)])

            def emit_tp(idx, ib):
                for kt in range(NSPAN):
                    tpu.t(pT_sb[ib][:, kt * 128:(kt + 1) * 128],
                          p_sb[ib][:, kt * 128:(kt + 1) * 128],
                          reads=[("p", ib)], writes=[("pT", ib)],
                          evac="vector" if kt % 2 else "scalar")

            for qq in range(8):
                qb = qq % 2
                s.dma("qt", lambda sy, qq=qq, qb=qb: sy.dma_start(
                        out=qt_sb[qb][:].rearrange("p (f t) -> p f t", f=32), in_=qTr_r[:, :, qq * 128:(qq + 1) * 128]),
                      writes=[("qt", qb)])
                for kt in range(NSPAN):
                    j0 = (qq + kt) * 128
                    s.op("vector", lambda v, qq=qq, j0=j0: v.tensor_scalar(
                            out=m1_sb[:], in0=kposb[:, j0:j0 + 128], scalar1=qpos_sb[:, qq:qq + 1],
                            scalar2=None, op0=ALU.is_le),
                         reads=[("kposb",), ("qpos", qq)], writes=[("m1",)])
                    s.op("vector", lambda v, qq=qq, j0=j0, kt=kt: v.scalar_tensor_tensor(
                            out=m1_sb[:], in0=kposb[:, j0:j0 + 128],
                            scalar=qpm_sb[:, qq:qq + 1], in1=m1_sb[:], op0=ALU.is_gt, op1=ALU.mult),
                         reads=[("kposb",), ("qpm",), ("m1",)], writes=[("m1",)])
                    s.op("vector", lambda v, kt=kt: v.tensor_scalar(
                            out=masks[:, kt * 128:(kt + 1) * 128], in0=m1_sb[:],
                            scalar1=400000.0, scalar2=-400000.0, op0=ALU.mult, op1=ALU.add),
                         reads=[("m1",)], writes=[("mask", kt)])
                for h in range(32):
                    idx = qq * 32 + h
                    ib = idx % 3
                    sb2 = idx % 2
                    kvh = h // 4
                    for sp in range(3):
                        w0 = qq * 128 + sp * 512
                        wlen = min(1152, (sp + 1) * 512) - sp * 512
                        s.op("tensor", lambda te, qb=qb, h=h, kvh=kvh, w0=w0, sp=sp, wlen=wlen: te.matmul(
                                psS[0][:, sp * 512: sp * 512 + wlen], qt_sb[qb][:, h * 128:(h + 1) * 128],
                                kwin[:, kvh, w0:w0 + wlen], start=True, stop=True),
                             reads=[("qt", qb), ("kwin",)], writes=[("psS", 0)])
                    s.op("vector", lambda v, ib=ib: v.tensor_tensor(
                            out=Ssb[ib][:], in0=psS[0][:, 0:1152], in1=masks[:], op=ALU.add),
                         reads=[("psS", 0)] + [("mask", kt) for kt in range(NSPAN)],
                         writes=[("Ss", ib)])
                    s.op("vector", lambda v, ib=ib: v.reduce_max(out=ngx[:, ib:ib + 1], in_=Ssb[ib][:],
                                                                 axis=AX.X, negate=True),
                         reads=[("Ss", ib)], writes=[("ngx", ib)])
                    s.op("vector", lambda v, ib=ib: v.tensor_scalar(out=ngx[:, ib:ib + 1], in0=ngx[:, ib:ib + 1],
                                                                    scalar1=SM, scalar2=None, op0=ALU.mult),
                         reads=[("ngx", ib)], writes=[("ngx", ib)])
                    s.op("scalar", lambda sc, ib=ib: sc.activation(
                            out=p_sb[ib][:], in_=Ssb[ib][:], func=AF.Exp, bias=ngx[:, ib:ib + 1], scale=SM,
                            accum_out=rs_sb[:, ib:ib + 1]),
                         reads=[("Ss", ib), ("ngx", ib)], writes=[("p", ib), ("rs", ib)])
                    pend_tp.append((idx, ib))
                    if len(pend_tp) > 1:
                        emit_tp(*pend_tp.pop(0))
                    pend.append((idx, qq, h, ib))
                    if len(pend) > 2:
                        emit_av(*pend.pop(0))
                while pend_tp:
                    emit_tp(*pend_tp.pop(0))
                while pend:
                    emit_av(*pend.pop(0))
                for dd in range(NC):
                    tpu.t(oT_all[:, dd * T + qq * 128: dd * T + (qq + 1) * 128],
                          o_bf[:, dd * 128:(dd + 1) * 128],
                          reads=[("o", dd)], writes=[("oT", qq, dd)],
                          evac="vector" if dd % 2 else "scalar")

        s.barrier()
        with ExitStack() as mctx:
            msb = lambda *a: mctx.enter_context(nc.sbuf_tensor(*a))
            A_all = msb("A_allE", [128, 8, 4096], BF16)
            ectx = mctx.enter_context(ExitStack())
            esb = lambda *a: ectx.enter_context(nc.sbuf_tensor(*a))
            wst = [esb(f"wstE{i}", [128, 8 * 512], F32) for i in range(2)]
            wo_bf = esb("wo_bfE", [128, 32 * 512], BF16)
            oT_ready = [("oT", qq, dd) for qq in range(8) for dd in range(NC)]
            for ch in range(8):
                for hf in range(4):
                    hp = hf % 2
                    s.dma(f"w{hp}", lambda sy, ch=ch, hf=hf, hp=hp: sy.dma_start(
                            out=wst[hp][:].rearrange("p (a b) -> p a b", a=8),
                            in_=wo_r[:, hf * 8:(hf + 1) * 8, ch * 512:(ch + 1) * 512]),
                          writes=[("wst", hp)])
                    s.op("scalar", lambda sc, hf=hf, hp=hp: sc.copy(
                            out=wo_bf[:, hf * 4096:(hf + 1) * 4096], in_=wst[hp][:]),
                         reads=[("wst", hp)], writes=[("wo_bf", hf)])
                for qq in range(8):
                    ab = qq % 2
                    for dd in range(NC):
                        s.op("tensor", lambda te, ab=ab, ch=ch, qq=qq, dd=dd: te.matmul(
                                psS[0][:, ab * 512:(ab + 1) * 512],
                                oT_all[:, dd * T + qq * 128: dd * T + (qq + 1) * 128],
                                wo_bf[:, dd * 512:(dd + 1) * 512],
                                start=(dd == 0), stop=(dd == NC - 1)),
                             reads=[("wo_bf", dd // 8)] + (oT_ready if (ch == 0 and qq == 0 and dd == 0) else []),
                             writes=[("psSo", ab)])
                    s.op("scalar", lambda sc, ab=ab, ch=ch, qq=qq: sc.copy(
                            out=A_all[:, qq, ch * 512:(ch + 1) * 512], in_=psS[0][:, ab * 512:(ab + 1) * 512]),
                         reads=[("psSo", ab)], writes=[("A", qq, ch)])
            ectx.close()
            s.barrier()
            fctx = mctx.enter_context(ExitStack())
            fsb = lambda *a: fctx.enter_context(nc.sbuf_tensor(*a))
            yt = fsb("ytE", [128, D], F32)
            lnt = fsb("lntE", [128, D], F32)
            stats = fsb("statsE", [128, 48], F32)
            mv = fsb("mvE", [128, 2], F32)
            rstd = fsb("rstdE", [128, 1], F32)
            negmb = fsb("negmbE", [128, 1], F32)
            for qq in range(8):
                s.dma("x", lambda sy, qq=qq: sy.dma_start(out=yt[:], in_=yd[qq * 128:(qq + 1) * 128, :]),
                      writes=[("yt",)])
                ln_prepare(s, A_all[:, qq, :], stats, mv, rstd, negmb,
                           [("A", qq, ch) for ch in range(8)], ("lnE", qq), ncols=D, col0=0)
                ln_apply(s, lnt[:], A_all[:, qq, :], rstd, negmb, ("lnE", qq),
                         [("A", qq, ch) for ch in range(8)], ("lnt",))
                s.op("vector", lambda v: v.tensor_tensor(out=yt[:], in0=yt[:], in1=lnt[:], op=ALU.add),
                     reads=[("yt",), ("lnt",)], writes=[("yt",)])
                s.dma("st", lambda sy, qq=qq: sy.dma_start(out=y3[qq * 128:(qq + 1) * 128, :], in_=yt[:]),
                      reads=[("yt",)], writes=[])
        if standalone:
            s.emit()
    return nc



def build_l1ffn(s=None, io=None):
    standalone = io is None
    if standalone:
        nc = bass.Bass("TRN2", target_bir_lowering=False, debug=False)
        yd = nc.dram_tensor("y", [T, D], F32, kind="ExternalInput").ap()
        w8 = nc.dram_tensor("w8", [D, D], I8, kind="ExternalInput").ap()
        scl = nc.dram_tensor("scl", [1, D], F32, kind="ExternalInput").ap()
        bias = nc.dram_tensor("bias", [D], F32, kind="ExternalInput").ap()
        yo = nc.dram_tensor("yo", [T, D], F32, kind="ExternalOutput").ap()
    else:
        nc = s.nc
        yd, w8, scl, bias, yo = (io[k] for k in ("y", "w8", "scl", "bias", "yo"))

    w8_r = w8.rearrange("(oc p) i -> p oc i", p=128)

    if standalone:
        s = Sched(nc)
    with ExitStack() as ctx:
        sb = lambda *a: ctx.enter_context(nc.sbuf_tensor(*a))
        ps = lambda *a: ctx.enter_context(nc.psum_tensor(*a))
        AT_all = sb("AT_allF", [128, NC * T], BF16)
        psF = [ps(f"psFF{i}", [128, 512], F32) for i in range(2)]
        scl_sb = sb("sclF", [128, 32], F32)
        bias_sb = sb("biasF", [128, 32], F32)

        with ExitStack() as actx:
            asb = lambda *a: actx.enter_context(nc.sbuf_tensor(*a))
            tpu = TPU(s, nc, actx)
            xt = asb("xtF", [128, D], F32)
            stats = asb("statsF", [128, 48], F32)
            mv = asb("mvF", [128, 2], F32)
            rstd = asb("rstdF", [128, 1], F32)
            negmb = asb("negmbF", [128, 1], F32)
            xn_bf = [asb(f"xn_bfF{i}", [128, D], BF16) for i in range(2)]
            xnT = asb("xnTF", [128, NC * T], BF16)
            w8st = [asb(f"w8stF{i}", [128, D], I8) for i in range(2)]
            w8bf = [asb(f"w8bfF{i}", [128, D], BF16) for i in range(2)]
            w8T = [asb(f"w8TF{i}", [128, NC * 128], BF16) for i in range(2)]

            for oc in range(32):
                s.dma("sc", lambda sy, oc=oc: sy.dma_start(
                        out=scl_sb[:, oc:oc + 1], in_=scl[0, oc * 128:(oc + 1) * 128].unsqueeze(-1)),
                      writes=[("scl", oc)])
                s.dma("sc", lambda sy, oc=oc: sy.dma_start(
                        out=bias_sb[:, oc:oc + 1], in_=bias[oc * 128:(oc + 1) * 128].unsqueeze(-1)),
                      writes=[("bias", oc)])

            for t in range(NT):
                tp = t % 2
                s.dma("x", lambda sy, t=t: sy.dma_start(out=xt[:], in_=yd[t * 128:(t + 1) * 128, :]),
                      writes=[("xt",)])
                ln_prepare(s, xt, stats, mv, rstd, negmb, [("xt",)], ("lnF", t))
                ln_apply(s, xn_bf[tp][:], xt[:], rstd, negmb, ("lnF", t), [("xt",)], ("xn_bf", tp))
                for cc in range(NC):
                    tpu.t(xnT[:, cc * T + t * 128: cc * T + (t + 1) * 128],
                          xn_bf[tp][:, cc * 128:(cc + 1) * 128],
                          reads=[("xn_bf", tp)], writes=[("xnT", t, cc)], evac="vector")
            xnT_ready = [("xnT", t, cc) for t in range(NT) for cc in range(NC)]

            first = True
            for oc in range(32):
                p = oc % 2
                s.dma(f"w{p}", lambda sy, oc=oc, p=p: sy.dma_start(out=w8st[p][:], in_=w8[oc * 128:(oc + 1) * 128, :]),
                      writes=[("w8st", p)])
                s.op("scalar", lambda sc, p=p: sc.copy(out=w8bf[p][:], in_=w8st[p][:]),
                     reads=[("w8st", p)], writes=[("w8bf", p)])
                for ic in range(NC):
                    tpu.t(w8T[p][:, ic * 128:(ic + 1) * 128], w8bf[p][:, ic * 128:(ic + 1) * 128],
                          reads=[("w8bf", p)], writes=[("w8T", p)], evac="vector")
                for ic in range(NC):
                    for th in range(2):
                        s.op("tensor", lambda te, p=p, th=th, ic=ic: te.matmul(
                                psF[th][:], w8T[p][:, ic * 128:(ic + 1) * 128],
                                xnT[:, ic * T + th * 512: ic * T + (th + 1) * 512],
                                start=(ic == 0), stop=(ic == NC - 1)),
                             reads=[("w8T", p)] + (xnT_ready if first else []),
                             writes=[("psF", th)], tag="mmF")
                        first = False
                for th in range(2):
                    s.op("scalar", lambda sc, oc=oc, th=th: sc.activation(
                            out=AT_all[:, oc * T + th * 512: oc * T + (th + 1) * 512], in_=psF[th][:],
                            func=AF.Identity, bias=bias_sb[:, oc:oc + 1], scale=scl_sb[:, oc:oc + 1]),
                         reads=[("psF", th), ("scl", oc), ("bias", oc)], writes=[("AT", oc, th)])

        s.barrier()
        mctx = ctx.enter_context(ExitStack())
        A_tok = mctx.enter_context(nc.sbuf_tensor("A_tokF", [128, 8, 4096], BF16))
        tpu2 = TPU(s, nc, mctx)
        for oc in range(32):
            for tt in range(NT):
                tpu2.t(A_tok[:, tt, oc * 128:(oc + 1) * 128],
                       AT_all[:, oc * T + tt * 128: oc * T + (tt + 1) * 128],
                       reads=[("AT", oc, 0), ("AT", oc, 1)], writes=[("A_tok", tt, oc)],
                       evac="scalar" if (oc + tt) % 2 else "vector")

        s.barrier()
        with ExitStack() as fctx:
            fsb = lambda *a: fctx.enter_context(nc.sbuf_tensor(*a))
            yt = fsb("ytF2", [128, D], F32)
            lnt = fsb("lntF2", [128, D], F32)
            stats = fsb("statsF2", [128, 48], F32)
            mv = fsb("mvF2", [128, 2], F32)
            rstd = fsb("rstdF2", [128, 1], F32)
            negmb = fsb("negmbF2", [128, 1], F32)
            for tt in range(NT):
                s.dma("x2", lambda sy, tt=tt: sy.dma_start(out=yt[:], in_=yd[tt * 128:(tt + 1) * 128, :]),
                      writes=[("yt",)])
                ln_prepare(s, A_tok[:, tt, :], stats, mv, rstd, negmb,
                           [("A_tok", tt, oc) for oc in range(32)], ("lnF2", tt))
                ln_apply(s, lnt[:], A_tok[:, tt, :], rstd, negmb, ("lnF2", tt),
                         [("A_tok", tt, oc) for oc in range(32)], ("lnt",))
                s.op("vector", lambda v: v.tensor_tensor(out=yt[:], in0=yt[:], in1=lnt[:], op=ALU.add),
                     reads=[("yt",), ("lnt",)], writes=[("yt",)])
                s.dma("st", lambda sy, tt=tt: sy.dma_start(out=yo[tt * 128:(tt + 1) * 128, :], in_=yt[:]),
                      reads=[("yt",)], writes=[])
        if standalone:
            s.emit()
    return nc



def build_bcd():
    nc = bass.Bass("TRN2", target_bir_lowering=False, debug=False)
    g = lambda n, shp, dt, kind: nc.dram_tensor(n, shp, dt, kind=kind).ap()
    x = g("x", [T, D], F32, "ExternalInput")
    qTd = g("qT", [1024, T], BF16, "ExternalInput")
    kTf = g("kTf", [1024, 8192], BF16, "ExternalInput")
    vf = g("vf", [8192, 1024], BF16, "ExternalInput")
    wo0 = g("wo", [1024, D], F32, "ExternalInput")
    bo = g("bo", [D], F32, "ExternalInput")
    rw = g("rw", [64, D], F32, "ExternalInput")
    rb = g("rb", [64], F32, "ExternalInput")
    eb = g("eb", [64, D], F32, "ExternalInput")
    wq = g("wq", [D, 4096], F32, "ExternalInput")
    wk = g("wk", [D, 1024], F32, "ExternalInput")
    wv = g("wv", [D, 1024], F32, "ExternalInput")
    pos = g("pos", [1, T], F32, "ExternalInput")
    y2 = g("y2", [T, D], F32, "ExternalOutput")
    qTr = g("qTr", [4096, T], BF16, "ExternalOutput")
    kTr = g("kTr", [1024, T], BF16, "ExternalOutput")
    vtok = g("vtok", [T, 1024], BF16, "ExternalOutput")
    y1 = nc.dram_tensor("y1i", [T, D], F32).ap()
    s = Sched(nc)
    build_l0attn(s=s, io=dict(x=x, qT=qTd, kTf=kTf, vf=vf, wo=wo0, bo=bo, y=y1))
    s.barrier()
    build_l0ffn(s=s, io=dict(y=y1, rw=rw, rb=rb, eb=eb, y2=y2))
    s.barrier()
    build_l1qkv(s=s, io=dict(y=y2, wq=wq, wk=wk, wv=wv, pos=pos, qTr=qTr, kTr=kTr, vtok=vtok))
    s.emit()
    return nc


def build_ef():
    nc = bass.Bass("TRN2", target_bir_lowering=False, debug=False)
    g = lambda n, shp, dt, kind: nc.dram_tensor(n, shp, dt, kind=kind).ap()
    y2 = g("y", [T, D], F32, "ExternalInput")
    qTr = g("qTr", [4096, T], BF16, "ExternalInput")
    kw = g("kw", [1024, 2048], BF16, "ExternalInput")
    vw = g("vw", [2048, 1024], BF16, "ExternalInput")
    wo1 = g("wo", [4096, D], F32, "ExternalInput")
    qpos = g("qpos", [1, T], F32, "ExternalInput")
    kpos = g("kpos", [1, 2048], F32, "ExternalInput")
    w8 = g("w8", [D, D], I8, "ExternalInput")
    scl = g("scl", [1, D], F32, "ExternalInput")
    bias = g("bias", [D], F32, "ExternalInput")
    yo = g("yo", [T, D], F32, "ExternalOutput")
    y3 = nc.dram_tensor("y3i", [T, D], F32).ap()
    s = Sched(nc)
    build_l1attn(s=s, io=dict(y=y2, qTr=qTr, kw=kw, vw=vw, wo=wo1, qpos=qpos, kpos=kpos, y3=y3))
    s.barrier()
    build_l1ffn(s=s, io=dict(y=y3, w8=w8, scl=scl, bias=bias, yo=yo))
    s.emit()
    return nc


_BUILT = {}


def _get(name, builder):
    if name not in _BUILT:
        _BUILT[name] = builder()
    return _BUILT[name]


def _run_l0qkv(x_full, inp):
    ncA = _get("l0qkv", build_l0qkv)
    wqkv = np.ascontiguousarray(inp["L0_attn_Wqkv"])
    bqkv = np.ascontiguousarray(inp["L0_attn_bqkv"])
    in_maps = [dict(x=np.ascontiguousarray(x_full[c * T:(c + 1) * T]), wqkv=wqkv, bqkv=bqkv)
               for c in range(NCORES)]
    res = run_bass_kernel_spmd(ncA, in_maps, core_ids=list(range(NCORES)))
    qT = [np.asarray(r["qT"]) for r in res.results]
    kT_full = np.ascontiguousarray(np.concatenate([np.asarray(r["kT"]) for r in res.results], axis=1))
    v_full = np.ascontiguousarray(np.concatenate([np.asarray(r["vtok"]) for r in res.results], axis=0))
    return qT, kT_full, v_full


def _run_l0attn(x_full, qT, kT_full, v_full, inp):
    ncB = _get("l0attn", build_l0attn)
    wo = np.ascontiguousarray(inp["L0_attn_Wo"])
    bo = np.ascontiguousarray(inp["L0_attn_bo"])
    in_maps = [dict(x=np.ascontiguousarray(x_full[c * T:(c + 1) * T]), qT=np.ascontiguousarray(qT[c]),
                    kTf=kT_full, vf=v_full, wo=wo, bo=bo)
               for c in range(NCORES)]
    res = run_bass_kernel_spmd(ncB, in_maps, core_ids=list(range(NCORES)))
    return np.concatenate([np.asarray(r["y"]) for r in res.results], axis=0)


def _run_l0ffn(y1, inp):
    ncC = _get("l0ffn", build_l0ffn)
    rw = np.ascontiguousarray(inp["L0_ffn_router_weight"])
    rb = np.ascontiguousarray(inp["L0_ffn_router_bias"])
    eb = np.ascontiguousarray(inp["L0_ffn_expert_bias"])
    in_maps = [dict(y=np.ascontiguousarray(y1[c * T:(c + 1) * T]), rw=rw, rb=rb, eb=eb)
               for c in range(NCORES)]
    res = run_bass_kernel_spmd(ncC, in_maps, core_ids=list(range(NCORES)))
    return np.concatenate([np.asarray(r["y2"]) for r in res.results], axis=0)


def _run_l1qkv(y2, inp):
    ncD = _get("l1qkv", build_l1qkv)
    wq = np.ascontiguousarray(inp["L1_attn_wq"])
    wk = np.ascontiguousarray(inp["L1_attn_wk"])
    wv = np.ascontiguousarray(inp["L1_attn_wv"])
    pos = np.arange(8192, dtype=np.float32)
    in_maps = [dict(y=np.ascontiguousarray(y2[c * T:(c + 1) * T]), wq=wq, wk=wk, wv=wv,
                    pos=np.ascontiguousarray(pos[c * T:(c + 1) * T].reshape(1, T)))
               for c in range(NCORES)]
    res = run_bass_kernel_spmd(ncD, in_maps, core_ids=list(range(NCORES)))
    qTr = [np.asarray(r["qTr"]) for r in res.results]
    kTr_full = np.ascontiguousarray(np.concatenate([np.asarray(r["kTr"]) for r in res.results], axis=1))
    v_full = np.ascontiguousarray(np.concatenate([np.asarray(r["vtok"]) for r in res.results], axis=0))
    return qTr, kTr_full, v_full


def _run_l1attn(y2, qTr, kTr_full, v_full, inp):
    ncE = _get("l1attn", build_l1attn)
    wo = np.ascontiguousarray(inp["L1_attn_wo"])
    pos = np.arange(8192, dtype=np.float32)
    in_maps = []
    for c in range(NCORES):
        if c == 0:
            kw = np.concatenate([kTr_full[:, -T:], kTr_full[:, :T]], axis=1)
            vw = np.concatenate([v_full[-T:, :], v_full[:T, :]], axis=0)
            kpos = np.concatenate([np.full(T, -1e6, np.float32), pos[:T]])
        else:
            kw = kTr_full[:, (c - 1) * T:(c + 1) * T]
            vw = v_full[(c - 1) * T:(c + 1) * T, :]
            kpos = pos[(c - 1) * T:(c + 1) * T]
        in_maps.append(dict(y=np.ascontiguousarray(y2[c * T:(c + 1) * T]),
                            qTr=np.ascontiguousarray(qTr[c]),
                            kw=np.ascontiguousarray(kw), vw=np.ascontiguousarray(vw), wo=wo,
                            qpos=np.ascontiguousarray(pos[c * T:(c + 1) * T].reshape(1, T)),
                            kpos=np.ascontiguousarray(kpos.reshape(1, 2 * T))))
    res = run_bass_kernel_spmd(ncE, in_maps, core_ids=list(range(NCORES)))
    return np.concatenate([np.asarray(r["y3"]) for r in res.results], axis=0)


def _run_l1ffn(y3, inp):
    ncF = _get("l1ffn", build_l1ffn)
    w8 = np.ascontiguousarray(inp["L1_ffn_weight_q"])
    scl = np.ascontiguousarray(inp["L1_ffn_scale"])
    bias = np.ascontiguousarray(inp["L1_ffn_bias"])
    in_maps = [dict(y=np.ascontiguousarray(y3[c * T:(c + 1) * T]), w8=w8, scl=scl, bias=bias)
               for c in range(NCORES)]
    res = run_bass_kernel_spmd(ncF, in_maps, core_ids=list(range(NCORES)))
    return np.concatenate([np.asarray(r["yo"]) for r in res.results], axis=0)


def _run_bcd(x_full, qT, kT_full, v_full, inp):
    ncB = _get("bcd", build_bcd)
    pos = np.arange(8192, dtype=np.float32)
    common = dict(wo=np.ascontiguousarray(inp["L0_attn_Wo"]), bo=np.ascontiguousarray(inp["L0_attn_bo"]),
                  rw=np.ascontiguousarray(inp["L0_ffn_router_weight"]), rb=np.ascontiguousarray(inp["L0_ffn_router_bias"]),
                  eb=np.ascontiguousarray(inp["L0_ffn_expert_bias"]), wq=np.ascontiguousarray(inp["L1_attn_wq"]),
                  wk=np.ascontiguousarray(inp["L1_attn_wk"]), wv=np.ascontiguousarray(inp["L1_attn_wv"]),
                  kTf=kT_full, vf=v_full)
    in_maps = [dict(x=np.ascontiguousarray(x_full[c * T:(c + 1) * T]), qT=np.ascontiguousarray(qT[c]),
                    pos=np.ascontiguousarray(pos[c * T:(c + 1) * T].reshape(1, T)), **common)
               for c in range(NCORES)]
    res = run_bass_kernel_spmd(ncB, in_maps, core_ids=list(range(NCORES)))
    y2 = np.concatenate([np.asarray(r["y2"]) for r in res.results], axis=0)
    qTr = [np.asarray(r["qTr"]) for r in res.results]
    kTr_full = np.ascontiguousarray(np.concatenate([np.asarray(r["kTr"]) for r in res.results], axis=1))
    vr_full = np.ascontiguousarray(np.concatenate([np.asarray(r["vtok"]) for r in res.results], axis=0))
    return y2, qTr, kTr_full, vr_full


def _run_ef(y2, qTr, kTr_full, vr_full, inp):
    ncE = _get("ef", build_ef)
    wo = np.ascontiguousarray(inp["L1_attn_wo"])
    w8 = np.ascontiguousarray(inp["L1_ffn_weight_q"])
    scl = np.ascontiguousarray(inp["L1_ffn_scale"])
    bias = np.ascontiguousarray(inp["L1_ffn_bias"])
    pos = np.arange(8192, dtype=np.float32)
    in_maps = []
    for c in range(NCORES):
        if c == 0:
            kwv = np.concatenate([kTr_full[:, -T:], kTr_full[:, :T]], axis=1)
            vwv = np.concatenate([vr_full[-T:, :], vr_full[:T, :]], axis=0)
            kposv = np.concatenate([np.full(T, -1e6, np.float32), pos[:T]])
        else:
            kwv = kTr_full[:, (c - 1) * T:(c + 1) * T]
            vwv = vr_full[(c - 1) * T:(c + 1) * T, :]
            kposv = pos[(c - 1) * T:(c + 1) * T]
        in_maps.append(dict(y=np.ascontiguousarray(y2[c * T:(c + 1) * T]),
                            qTr=np.ascontiguousarray(qTr[c]),
                            kw=np.ascontiguousarray(kwv), vw=np.ascontiguousarray(vwv), wo=wo,
                            qpos=np.ascontiguousarray(pos[c * T:(c + 1) * T].reshape(1, T)),
                            kpos=np.ascontiguousarray(kposv.reshape(1, 2 * T)),
                            w8=w8, scl=scl, bias=bias))
    res = run_bass_kernel_spmd(ncE, in_maps, core_ids=list(range(NCORES)))
    return np.concatenate([np.asarray(r["yo"]) for r in res.results], axis=0)


def kernel(**inputs):
    inp = {k: np.asarray(v) for k, v in inputs.items()}
    x_full = np.ascontiguousarray(inp["L0_attn_x"].reshape(8192, D))

    qT, kT_full, v_full = _run_l0qkv(x_full, inp)
    y2, qTr, kTr_full, vr_full = _run_bcd(x_full, qT, kT_full, v_full, inp)
    yo = _run_ef(y2, qTr, kTr_full, vr_full, inp)
    return yo.reshape(1, 8192, D).astype(np.float32)
